# version 04
# baseline (speedup 1.0000x reference)
from contextlib import ExitStack

import numpy as np
import ml_dtypes

import concourse.bass as bass
import concourse.mybir as mybir
from concourse.bass_utils import run_bass_kernel_spmd

F32 = mybir.dt.float32
BF16 = mybir.dt.bfloat16
I32 = mybir.dt.int32
OP = mybir.AluOpType
AF = mybir.ActivationFunctionType
AX = mybir.AxisListType

NT = 4096
D = 2048
NF = 16
MLP = 16384
NNC = 128
H = 16
DH = 128
NTT = 32
NG = 8
GELU_C1 = 0.7978845608028654
GELU_C2 = GELU_C1 * 0.044715
TANH_SC = (DH ** -0.5) / 50.0
EXP_BIAS = -30.0
LN_EPS = 1e-5
NCORES = 8
NK1 = 128
P1S = 5
P1EVAC = True
PREP_ONLY = False


class S:
    def __init__(self, h):
        self.h = h
        self.n = 0

    def inc(self, ins, k=1):
        ins.then_inc(self.h, k)
        self.n += k
        return ins

    def wait(self, eng, tgt=None):
        t = self.n if tgt is None else tgt
        if t > 0:
            eng.wait_ge(self.h, t)


def interleave(*lists):
    n = max((len(l) for l in lists), default=0)
    for i in range(n):
        for l in lists:
            if i < len(l):
                l[i]()


def build_nc(dbg=(), phases=6):
    nc = bass.Bass("TRN2", target_bir_lowering=False, disable_frame_to_traceback=True)

    def inp(name, shape, dt):
        return nc.dram_tensor(name, shape, dt, kind="ExternalInput")

    def scratch(name, shape, dt):
        kind = "ExternalOutput" if name in dbg else "Internal"
        return nc.dram_tensor(name, shape, dt, kind=kind)

    x0 = inp("x0", [NT, D], F32)
    wqkv = inp("wqkv", [D, 3 * H * DH], F32)
    wgates = inp("wgates", [D, H], F32)
    wout = inp("wout", [D, D], F32)
    gW = inp("gW", [MLP, D // 2], I32)
    gS = inp("gS", [MLP, D // 32], F32)
    uW = inp("uW", [MLP, D // 2], I32)
    uS = inp("uS", [MLP, D // 32], F32)
    dW = inp("dW", [D, MLP // 2], I32)
    dS = inp("dS", [D, MLP // 32], F32)
    ident16 = inp("ident16", [128, 128], BF16)
    identf = inp("identf", [128, 128], F32)
    mask = inp("mask", [128, 128], F32)
    cbias = inp("cbias", [128, 8], F32)

    out = nc.dram_tensor("out", [NT, D], F32, kind="ExternalOutput")

    x1d = scratch("x1d", [NT, D], F32)
    x2d = scratch("x2d", [NT, D], F32)
    wgT = scratch("wgT", [NNC, 128, D], BF16)
    wuT = scratch("wuT", [NNC, 128, D], BF16)
    wdT = scratch("wdT", [NF, MLP, 128], BF16)
    interd = scratch("interd", [NG, NNC, 128, 512], BF16)
    qhd = scratch("qhd", [H, 128, NT], BF16)
    khd = scratch("khd", [H, 128, NT], BF16)
    vhd = scratch("vhd", [H, NT, 128], BF16)

    top = ExitStack()
    with top:
        idt = top.enter_context(nc.sbuf_tensor("idt", [128, 128], BF16))
        idtf = top.enter_context(nc.sbuf_tensor("idtf", [128, 128], F32))
        msk = top.enter_context(nc.sbuf_tensor("msk", [128, 128], F32))
        cb = top.enter_context(nc.sbuf_tensor("cb", [128, 8], F32))
        gates_tm = top.enter_context(nc.sbuf_tensor("gates_tm", [128, NTT * H], F32))

        qL = S(top.enter_context(nc.semaphore("qL")))
        qG = S(top.enter_context(nc.semaphore("qG")))
        sV = S(top.enter_context(nc.semaphore("sV")))
        sS = S(top.enter_context(nc.semaphore("sS")))
        sT = S(top.enter_context(nc.semaphore("sT")))
        qI = S(top.enter_context(nc.semaphore("qI")))
        dA = [S(top.enter_context(nc.semaphore(f"dA{i}"))) for i in range(2)]
        dB = [S(top.enter_context(nc.semaphore(f"dB{i}"))) for i in range(2)]
        dC = S(top.enter_context(nc.semaphore("dC")))
        dQ = [S(top.enter_context(nc.semaphore(f"dQ{i}"))) for i in range(4)]
        dU = [S(top.enter_context(nc.semaphore(f"dU{i}"))) for i in range(2)]
        ALL = [qL, qG, sV, sS, sT, qI] + dA + dB + [dC] + dQ + dU

        block = top.enter_context(nc.Block())

        def barrier():
            tgts = [(s, s.n) for s in ALL]
            for emit in (block.sync, block.gpsimd, block.vector, block.scalar, block.tensor):
                def f(eng, tgts=tgts):
                    for s, t in tgts:
                        s.wait(eng, t)
                emit(f)

        def f(sync):
            qL.inc(sync.dma_start(out=idt[:], in_=ident16.ap()), 16)
            qL.inc(sync.dma_start(out=idtf[:], in_=identf.ap()), 16)
            qL.inc(sync.dma_start(out=msk[:], in_=mask.ap()), 16)
            qL.inc(sync.dma_start(out=cb[:], in_=cbias.ap()), 16)
        block.sync(f)
        barrier()

        def prep_y(src, ybuf, t0, ntiles, tokw, es, tag):
            xt = [es.enter_context(nc.sbuf_tensor(f"px{tag}{i}", [128, D], F32)) for i in range(2)]
            sq = [es.enter_context(nc.sbuf_tensor(f"pq{tag}{i}", [128, D], F32)) for i in range(2)]
            yt = [es.enter_context(nc.sbuf_tensor(f"py{tag}{i}", [128, D], BF16)) for i in range(2)]
            st = es.enter_context(nc.sbuf_tensor(f"ps{tag}", [128, 16], F32))
            psT = [es.enter_context(nc.psum_tensor(f"pp{tag}{i}", [128, 1024], BF16)) for i in range(2)]

            load_done = [0] * ntiles
            stats_done = [0] * ntiles
            rstd_done = [0] * ntiles
            norm_done = [0] * ntiles
            tr_done = [[0, 0] for _ in range(ntiles)]
            evac_done = [[0, 0] for _ in range(ntiles)]

            for i in range(ntiles):
                p = i % 2
                trow = (t0 + i) * 128

                def fsy(sync, i=i, p=p, trow=trow):
                    if i >= 2:
                        sV.wait(sync, norm_done[i - 2])
                    dA[p].inc(sync.dma_start(out=xt[p][:], in_=src[trow:trow + 128, :]), 16)
                    load_done[i] = dA[p].n
                block.sync(fsy)

                def fv1(v, i=i, p=p):
                    dA[p].wait(v, load_done[i])
                    c = p * 8
                    v.scalar_tensor_tensor(sq[p][:], xt[p][:], 1.0, xt[p][:], OP.bypass, OP.mult,
                                           accum_out=st[:, c + 0:c + 1])
                    sV.inc(v.reduce_sum(st[:, c + 1:c + 2], xt[p][:], axis=AX.X))
                    sV.wait(v)
                    v.tensor_scalar(st[:, c + 2:c + 3], st[:, c + 1:c + 2], 1.0 / D, None, OP.mult)
                    sV.inc(v.tensor_scalar(st[:, c + 4:c + 5], st[:, c + 0:c + 1], 1.0 / D, None, OP.mult))
                    sV.wait(v)
                    sV.inc(v.tensor_scalar(st[:, c + 5:c + 6], st[:, c + 2:c + 3], st[:, c + 2:c + 3], None, OP.mult))
                    sV.wait(v)
                    sV.inc(v.tensor_tensor(st[:, c + 6:c + 7], st[:, c + 4:c + 5], st[:, c + 5:c + 6], OP.subtract))
                    stats_done[i] = sV.n
                block.vector(fv1)

                def fsc(s, i=i, p=p):
                    sV.wait(s, stats_done[i])
                    c = p * 8
                    sS.inc(s.activation(st[:, c + 7:c + 8], st[:, c + 6:c + 7], AF.Ln, bias=cb[:, 1:2]))
                    sS.wait(s)
                    sS.inc(s.activation(st[:, c + 7:c + 8], st[:, c + 7:c + 8], AF.Exp, scale=-0.5))
                    rstd_done[i] = sS.n
                block.scalar(fsc)

                def fv2(v, i=i, p=p):
                    sS.wait(v, rstd_done[i])
                    if i >= 2:
                        sT.wait(v, tr_done[i - 2][1])
                    c = p * 8
                    sV.inc(v.tensor_scalar(yt[p][:], xt[p][:], st[:, c + 2:c + 3], st[:, c + 7:c + 8],
                                           OP.subtract, OP.mult))
                    norm_done[i] = sV.n
                block.vector(fv2)

                def ft(t, i=i, p=p):
                    sV.wait(t, norm_done[i])
                    for half in range(2):
                        if i >= 1:
                            sV.wait(t, evac_done[i - 1][half])
                        ins = None
                        for c8 in range(8):
                            c = half * 8 + c8
                            ins = t.transpose(psT[half][:, c8 * 128:(c8 + 1) * 128],
                                              yt[p][:, c * 128:(c + 1) * 128], idt[:])
                        sT.inc(ins)
                        tr_done[i][half] = sT.n
                block.tensor(ft)

                def fv3(v, i=i, p=p):
                    for half in range(2):
                        sT.wait(v, tr_done[i][half])
                        cbase = half * 8
                        dst = bass.AP(ybuf, cbase * tokw + i * 128,
                                      [[NF * tokw, 128], [tokw, 8], [1, 128]])
                        srcp = bass.AP(psT[half], 0, [[1024, 128], [128, 8], [1, 128]])
                        sV.inc(v.tensor_copy(dst, srcp))
                        evac_done[i][half] = sV.n
                block.vector(fv3)

        def decode_thunks(v, w_ap, sct_ap, A, B, T, sc4, nw, waits=()):
            nsc = nw // 16
            def t0():
                for fn in waits:
                    fn(v)
                v.tensor_scalar(sc4[:], sct_ap, 4.0, None, OP.mult)
            def t1():
                v.tensor_scalar(A[:], w_ap, 7, 6, OP.bitwise_and, OP.logical_shift_left)
            def t2():
                v.tensor_scalar(B[:], w_ap, 0x70, 18, OP.bitwise_and, OP.logical_shift_left)
            def t3():
                v.tensor_tensor(A[:], A[:], B[:], OP.bitwise_or)
            def t4():
                v.tensor_scalar(A[:], A[:], 0x3E003E00, None, OP.bitwise_or)
            def t5():
                v.tensor_scalar(T[:], A[:].bitcast(BF16), 2.0, -0.25, OP.mult, OP.add)
            def t6():
                v.tensor_tensor(T[:], T[:], A[:].bitcast(BF16), OP.min)
            def t7():
                v.tensor_scalar(A[:], w_ap, 8, 12, OP.bitwise_and, OP.logical_shift_left)
            def t8():
                v.tensor_scalar(B[:], w_ap, 0x80, 24, OP.bitwise_and, OP.logical_shift_left)
            def t9():
                v.tensor_tensor(B[:], B[:], A[:], OP.bitwise_or)
            def t10():
                v.tensor_tensor(B[:], T[:].bitcast(I32), B[:], OP.bitwise_or)
            def t11():
                scb = bass.AP(sc4, 0, [[nsc, 128], [1, nsc], [0, 32]])
                sV.inc(v.tensor_tensor(T[:], B[:].bitcast(BF16), scb, OP.mult))
            return [t0, t1, t2, t3, t4, t5, t6, t7, t8, t9, t10, t11]

        def ffn_gate_up(hf, es, layer):
            xsrc = x0 if layer == 0 else x2d
            tokw = 2048
            ybuf = es.enter_context(nc.sbuf_tensor(f"y{layer}_{hf}", [128, NF * tokw], BF16))
            with ExitStack() as pes:
                prep_y(xsrc, ybuf, hf * 16, 16, tokw, pes, f"a{layer}{hf}")
                barrier()

            if PREP_ONLY:
                return
            dec = layer == 0
            if dec:
                gwp = [es.enter_context(nc.sbuf_tensor(f"gwp{layer}{hf}{i}", [128, 1024], I32)) for i in range(2)]
                gsp = [es.enter_context(nc.sbuf_tensor(f"gsp{layer}{hf}{i}", [128, 64], F32)) for i in range(2)]
                uwp = [es.enter_context(nc.sbuf_tensor(f"uwp{layer}{hf}{i}", [128, 1024], I32)) for i in range(2)]
                usp = [es.enter_context(nc.sbuf_tensor(f"usp{layer}{hf}{i}", [128, 64], F32)) for i in range(2)]
                dwp = [es.enter_context(nc.sbuf_tensor(f"dwp{layer}{hf}{i}", [128, 1024], I32)) for i in range(2)]
                dsp = [es.enter_context(nc.sbuf_tensor(f"dsp{layer}{hf}{i}", [128, 64], F32)) for i in range(2)]
                Ag = es.enter_context(nc.sbuf_tensor(f"Ag{layer}{hf}", [128, 1024], I32))
                Bg = es.enter_context(nc.sbuf_tensor(f"Bg{layer}{hf}", [128, 1024], I32))
                Au = es.enter_context(nc.sbuf_tensor(f"Au{layer}{hf}", [128, 1024], I32))
                Bu = es.enter_context(nc.sbuf_tensor(f"Bu{layer}{hf}", [128, 1024], I32))
                Ad = es.enter_context(nc.sbuf_tensor(f"Ad{layer}{hf}", [128, 1024], I32))
                Bd = es.enter_context(nc.sbuf_tensor(f"Bd{layer}{hf}", [128, 1024], I32))
                Tg = [es.enter_context(nc.sbuf_tensor(f"Tg{layer}{hf}{i}", [128, 2048], BF16)) for i in range(2)]
                Tu = [es.enter_context(nc.sbuf_tensor(f"Tu{layer}{hf}{i}", [128, 2048], BF16)) for i in range(2)]
                Td = [es.enter_context(nc.sbuf_tensor(f"Td{layer}{hf}{i}", [128, 2048], BF16)) for i in range(2)]
                sc4g = es.enter_context(nc.sbuf_tensor(f"s4g{layer}{hf}", [128, 64], F32))
                sc4u = es.enter_context(nc.sbuf_tensor(f"s4u{layer}{hf}", [128, 64], F32))
                sc4d = es.enter_context(nc.sbuf_tensor(f"s4d{layer}{hf}", [128, 64], F32))
                wdu = [es.enter_context(nc.sbuf_tensor(f"wdu{layer}{hf}{i}", [128, 2048], BF16)) for i in range(2)]
            wgsb = [es.enter_context(nc.sbuf_tensor(f"wgs{layer}{hf}{i}", [128, 2048], BF16)) for i in range(2)]
            wusb = [es.enter_context(nc.sbuf_tensor(f"wus{layer}{hf}{i}", [128, 2048], BF16)) for i in range(2)]
            gsb = [es.enter_context(nc.sbuf_tensor(f"gsb{layer}{hf}{i}", [128, 512], F32)) for i in range(2)]
            sqsb = [es.enter_context(nc.sbuf_tensor(f"sqb{layer}{hf}{i}", [128, 512], F32)) for i in range(2)]
            hbuf = [es.enter_context(nc.sbuf_tensor(f"hb{layer}{hf}{i}", [128, 512], F32)) for i in range(2)]
            tnh = [es.enter_context(nc.sbuf_tensor(f"th{layer}{hf}{i}", [128, 512], F32)) for i in range(2)]
            u1b = [es.enter_context(nc.sbuf_tensor(f"u1{layer}{hf}{i}", [128, 512], F32)) for i in range(2)]
            isb = [es.enter_context(nc.sbuf_tensor(f"ib{layer}{hf}{i}", [128, 2048], BF16)) for i in range(2)]
            psT = [es.enter_context(nc.psum_tensor(f"1t{layer}{hf}{i}", [128, 1024], BF16)) for i in range(2)]
            psG = [es.enter_context(nc.psum_tensor(f"1g{layer}{hf}{i}", [128, 512], F32)) for i in range(2)]
            psU = [es.enter_context(nc.psum_tensor(f"1u{layer}{hf}{i}", [128, 512], F32)) for i in range(2)]

            NK = NK1
            load_done = [0] * NK
            dec_done = [0] * NK
            trg_done = [dict() for _ in range(NK)]
            evac_done = [dict() for _ in range(NK)]
            wst_done = [0] * NK
            mm_done = [[0] * 4 for _ in range(NK)]
            b_done = [[0] * 4 for _ in range(NK)]
            th_done = [[0] * 4 for _ in range(NK)]
            gelu_done = [[0] * 4 for _ in range(NK)]
            ist_done = [[0] * 4 for _ in range(NK)]

            for k in range(NK):
                p = k % 2
                do_d = dec and (k % 2 == 0)
                uk = hf * 64 + k // 2
                hc, ws = uk % NF, uk // NF
                dp = (k // 2) % 2

                def fsy(sync, k=k, p=p, do_d=do_d, hc=hc, ws=ws, dp=dp):
                    if k >= 2:
                        if dec:
                            sV.wait(sync, dec_done[k - 2])
                        else:
                            sT.wait(sync, mm_done[k - 2][3])
                    if dec:
                        dB[p].inc(sync.dma_start(out=gwp[p][:], in_=gW[k * 128:(k + 1) * 128, :]), 16)
                        dB[p].inc(sync.dma_start(out=gsp[p][:], in_=gS[k * 128:(k + 1) * 128, :]), 16)
                        dB[p].inc(sync.dma_start(out=uwp[p][:], in_=uW[k * 128:(k + 1) * 128, :]), 16)
                        dB[p].inc(sync.dma_start(out=usp[p][:], in_=uS[k * 128:(k + 1) * 128, :]), 16)
                        if do_d:
                            dB[p].inc(sync.dma_start(out=dwp[dp][:],
                                                  in_=dW[hc * 128:(hc + 1) * 128, ws * 1024:(ws + 1) * 1024]), 16)
                            dB[p].inc(sync.dma_start(out=dsp[dp][:],
                                                  in_=dS[hc * 128:(hc + 1) * 128, ws * 64:(ws + 1) * 64]), 16)
                    else:
                        dB[p].inc(sync.dma_start(out=wgsb[p][:], in_=wgT[k]), 16)
                        dB[p].inc(sync.dma_start(out=wusb[p][:], in_=wuT[k]), 16)
                    load_done[k] = dB[p].n
                block.sync(fsy)

                if dec:
                    def fvd(v, k=k, p=p, do_d=do_d, dp=dp):
                        def w0(vv, k=k, p=p):
                            dB[p].wait(vv, load_done[k])
                            if k >= 2:
                                sT.wait(vv, trg_done[k - 2].get("last", 0))
                        chains = [
                            decode_thunks(v, gwp[p][:], gsp[p][:], Ag, Bg, Tg[p], sc4g, 1024, waits=(w0,)),
                            decode_thunks(v, uwp[p][:], usp[p][:], Au, Bu, Tu[p], sc4u, 1024),
                        ]
                        if do_d:
                            chains.append(decode_thunks(v, dwp[dp][:], dsp[dp][:], Ad, Bd, Td[dp], sc4d, 1024))
                        interleave(*chains)
                        dec_done[k] = sV.n
                    block.vector(fvd)

                if dec and P1S >= 2:
                    seq = [("g", 0), ("g", 1), ("u", 0), ("u", 1)] +                           ([("d", 0), ("d", 1)] if do_d else [])
                    Tsrc = {"g": Tg[p], "u": Tu[p], "d": Td[dp] if do_d else None}
                    Edst = {"g": wgsb[p], "u": wusb[p], "d": wdu[dp] if do_d else None}
                    for j0 in range(0, len(seq), 2):
                        def ftt(t, k=k, j0=j0, seq=seq, Tsrc=Tsrc):
                            for j in (j0, j0 + 1):
                                mname, half = seq[j]
                                Tm = Tsrc[mname]
                                if j == 0:
                                    sV.wait(t, dec_done[k])
                                if j >= 2:
                                    sS.wait(t, evac_done[k].get(seq[j - 2], 0))
                                elif k >= 1:
                                    sS.wait(t, evac_done[k - 1].get("last2" if j == 0 else "last1", 0))
                                ins = None
                                for c8 in range(8):
                                    c = half * 8 + c8
                                    ins = t.transpose(psT[half][:, c8 * 128:(c8 + 1) * 128],
                                                      Tm[:, c * 128:(c + 1) * 128], idt[:])
                                sT.inc(ins)
                                trg_done[k][(mname, half)] = sT.n
                                trg_done[k]["last"] = sT.n
                        block.tensor(ftt)

                        if P1EVAC:
                            def fsev(s2, k=k, j0=j0, seq=seq, Edst=Edst):
                                for j in (j0, j0 + 1):
                                    mname, half = seq[j]
                                    dstbuf = Edst[mname]
                                    if mname != "d" and k >= 2 and half == 0:
                                        dU[k % 2].wait(s2, wst_done[k - 2])
                                        sT.wait(s2, mm_done[k - 2][3])
                                    if mname == "d" and k >= 4 and half == 0:
                                        dU[k % 2].wait(s2, wst_done[k - 2])
                                    sT.wait(s2, trg_done[k][(mname, half)])
                                    sS.inc(s2.copy(dstbuf[:, half * 1024:(half + 1) * 1024], psT[half][:]))
                                    evac_done[k][(mname, half)] = sS.n
                            block.scalar(fsev)
                    evac_done[k]["last2"] = evac_done[k].get(seq[-2], 0)
                    evac_done[k]["last1"] = evac_done[k].get(seq[-1], 0)

                for pr in (range(2) if P1S >= 3 else []):
                    g4a, g4b = pr * 2, pr * 2 + 1

                    def ftm(t, k=k, p=p, pr=pr, g4a=g4a, g4b=g4b):
                        if pr == 0:
                            if dec:
                                sS.wait(t, evac_done[k][("u", 1)])
                            else:
                                dB[p].wait(t, load_done[k])
                        for g4 in (g4a, g4b):
                            pg = g4 % 2
                            if g4 >= 2:
                                sV.wait(t, gelu_done[k][g4 - 2])
                            elif k >= 1:
                                sV.wait(t, gelu_done[k - 1][g4 + 2])
                            for c in range(NF):
                                t.matmul(psG[pg][:], wgsb[p][:, c * 128:(c + 1) * 128],
                                         ybuf[:, c * tokw + g4 * 512:c * tokw + (g4 + 1) * 512],
                                         start=(c == 0), stop=(c == NF - 1))
                            ins = None
                            for c in range(NF):
                                ins = t.matmul(psU[pg][:], wusb[p][:, c * 128:(c + 1) * 128],
                                               ybuf[:, c * tokw + g4 * 512:c * tokw + (g4 + 1) * 512],
                                               start=(c == 0), stop=(c == NF - 1))
                            sT.inc(ins)
                            mm_done[k][g4] = sT.n
                    block.tensor(ftm)
                    if P1S < 4:
                        continue

                    def fs1(s, k=k, g4a=g4a, g4b=g4b):
                        for g4 in (g4a, g4b):
                            pg = g4 % 2
                            sT.wait(s, mm_done[k][g4])
                            if g4 >= 2:
                                sV.wait(s, gelu_done[k][g4 - 2])
                            elif k >= 1:
                                sV.wait(s, gelu_done[k - 1][g4 + 2])
                            sS.inc(s.activation(sqsb[pg][:], psG[pg][:], AF.Square))
                            sS.inc(s.copy(gsb[pg][:], psG[pg][:]))
                    block.scalar(fs1)
                    sq_pair = sS.n

                    def fv1g(v, k=k, g4a=g4a, g4b=g4b, sq_pair=sq_pair):
                        sS.wait(v, sq_pair)
                        pa, pb = g4a % 2, g4b % 2
                        v.tensor_scalar(hbuf[pa][:], sqsb[pa][:], GELU_C2, GELU_C1, OP.mult, OP.add)
                        v.tensor_scalar(hbuf[pb][:], sqsb[pb][:], GELU_C2, GELU_C1, OP.mult, OP.add)
                        sV.inc(v.tensor_tensor(hbuf[pa][:], hbuf[pa][:], gsb[pa][:], OP.mult))
                        b_done[k][g4a] = sV.n
                        sV.inc(v.tensor_tensor(hbuf[pb][:], hbuf[pb][:], gsb[pb][:], OP.mult))
                        b_done[k][g4b] = sV.n
                    block.vector(fv1g)

                    def fs2(s, k=k, g4a=g4a, g4b=g4b):
                        for g4 in (g4a, g4b):
                            pg = g4 % 2
                            sV.wait(s, b_done[k][g4])
                            sS.inc(s.activation(tnh[pg][:], hbuf[pg][:], AF.Tanh))
                            th_done[k][g4] = sS.n
                    block.scalar(fs2)

                    def fv2g(v, k=k, g4a=g4a, g4b=g4b):
                        pa, pb = g4a % 2, g4b % 2
                        ip = k % 2
                        sS.wait(v, th_done[k][g4a])
                        if k >= 2 and g4a == 0:
                            dQ[ip].wait(v, ist_done[k - 2][0])
                        v.scalar_tensor_tensor(u1b[pa][:], tnh[pa][:], 1.0, gsb[pa][:], OP.add, OP.mult)
                        sS.wait(v, th_done[k][g4b])
                        v.scalar_tensor_tensor(u1b[pb][:], tnh[pb][:], 1.0, gsb[pb][:], OP.add, OP.mult)
                        sV.inc(v.scalar_tensor_tensor(isb[ip][:, g4a * 512:(g4a + 1) * 512],
                                                      u1b[pa][:], 0.5, psU[pa][:], OP.mult, OP.mult))
                        gelu_done[k][g4a] = sV.n
                        sV.inc(v.scalar_tensor_tensor(isb[ip][:, g4b * 512:(g4b + 1) * 512],
                                                      u1b[pb][:], 0.5, psU[pb][:], OP.mult, OP.mult))
                        gelu_done[k][g4b] = sV.n
                    block.vector(fv2g)

                    if pr == 1:
                        def fgp(gp, k=k):
                            if P1S >= 5:
                                sV.wait(gp, gelu_done[k][3])
                                ip = k % 2
                                dstp = bass.AP(interd, (hf * 4) * NNC * 128 * 512 + k * 128 * 512,
                                               [[512, 128], [NNC * 128 * 512, 4], [1, 512]])
                                srcp = bass.AP(isb[ip], 0, [[2048, 128], [512, 4], [1, 512]])
                                dQ[ip].inc(gp.dma_start(out=dstp, in_=srcp), 16)
                                ist_done[k][0] = dQ[ip].n
                        block.gpsimd(fgp)

                if dec and P1S >= 5:
                    def fgw(gp, k=k, p=p, do_d=do_d, hc=hc, ws=ws, dp=dp):
                        sS.wait(gp, evac_done[k][("u", 1)])
                        dU[p].inc(gp.dma_start(out=wgT[k], in_=wgsb[p][:]), 16)
                        dU[p].inc(gp.dma_start(out=wuT[k], in_=wusb[p][:]), 16)
                        if do_d:
                            sS.wait(gp, evac_done[k][("d", 1)])
                            srcp = bass.AP(wdu[dp], 0, [[2048, 128], [128, 16], [1, 128]])
                            dstp = bass.AP(wdT, hc * MLP * 128 + (ws * 2048) * 128,
                                           [[128, 128], [128 * 128, 16], [1, 128]])
                            dU[p].inc(gp.dma_start(out=dstp, in_=srcp), 16)
                        wst_done[k] = dU[p].n
                    block.gpsimd(fgw)

            barrier()

        def ffn_down(xres, dstd, layer, es):
            isb2 = es.enter_context(nc.sbuf_tensor(f"di{layer}", [128, NNC * 512], BF16))
            wdr = [es.enter_context(nc.sbuf_tensor(f"dw{layer}{i}", [128, 2048], BF16)) for i in range(2)]
            xrt = [es.enter_context(nc.sbuf_tensor(f"dx{layer}{i}", [128, 1024], F32)) for i in range(4)]
            x1s = [es.enter_context(nc.sbuf_tensor(f"dr{layer}{i}", [128, 512], F32)) for i in range(8)]
            psD = [es.enter_context(nc.psum_tensor(f"dp{layer}{i}", [128, 512], F32)) for i in range(8)]

            NK2 = NNC // 2
            prev_wdmm = None
            for tg in range(NG):
                def fsy0(sync, tg=tg):
                    if tg >= 1:
                        sT.wait(sync)
                    srcp = bass.AP(interd, tg * NNC * 128 * 512, [[512, 128], [128 * 512, NNC], [1, 512]])
                    dstp = bass.AP(isb2, 0, [[NNC * 512, 128], [512, NNC], [1, 512]])
                    qI.inc(sync.dma_start(out=dstp, in_=srcp), 16)
                block.sync(fsy0)
                inter_done = qI.n

                for hh in range(2):
                    wdld = [0] * NK2
                    wdmm = [0] * NK2
                    pw = prev_wdmm
                    for k2 in range(NK2):
                        def fsy(sync, tg=tg, hh=hh, k2=k2, pw=pw):
                            if k2 >= 2:
                                sT.wait(sync, wdmm[k2 - 2])
                            elif pw is not None:
                                sT.wait(sync, pw[NK2 - 2 + k2])
                            for ksub in range(2):
                                srcp = bass.AP(wdT, (hh * 8) * MLP * 128 + (k2 * 2 + ksub) * 128 * 128,
                                               [[128, 128], [MLP * 128, 8], [1, 128]])
                                dstp = bass.AP(wdr[k2 % 2], ksub * 1024, [[2048, 128], [128, 8], [1, 128]])
                                dB[k2 % 2].inc(sync.dma_start(out=dstp, in_=srcp), 16)
                            wdld[k2] = dB[k2 % 2].n
                        block.sync(fsy)

                        def ftm(t, tg=tg, hh=hh, k2=k2, inter_done=inter_done):
                            if k2 == 0:
                                qI.wait(t, inter_done)
                                sV.wait(t)
                            dB[k2 % 2].wait(t, wdld[k2])
                            ins = None
                            for ksub in range(2):
                                k = k2 * 2 + ksub
                                for tt in range(4):
                                    for hq in range(2):
                                        ins = t.matmul(psD[tt * 2 + hq][:],
                                                       isb2[:, k * 512 + tt * 128:k * 512 + (tt + 1) * 128],
                                                       wdr[k2 % 2][:, ksub * 1024 + hq * 512:ksub * 1024 + (hq + 1) * 512],
                                                       start=(k == 0), stop=(k == NNC - 1))
                            sT.inc(ins)
                            wdmm[k2] = sT.n
                        block.tensor(ftm)
                    prev_wdmm = wdmm

                    def fsyx(sync, tg=tg, hh=hh):
                        if tg >= 1 or hh >= 1:
                            sV.wait(sync)
                        for tt in range(4):
                            dC.inc(sync.dma_start(
                                out=xrt[tt][:],
                                in_=xres[tg * 512 + tt * 128:tg * 512 + (tt + 1) * 128,
                                         hh * 1024:(hh + 1) * 1024]), 16)
                    block.sync(fsyx)
                    xr_done = dC.n

                    def fve(v, tg=tg, hh=hh, xr_done=xr_done):
                        sT.wait(v, wdmm[NK2 - 1])
                        dC.wait(v, xr_done)
                        qG.wait(v)
                        for tt in range(4):
                            for hq in range(2):
                                i = tt * 2 + hq
                                sV.inc(v.scalar_tensor_tensor(
                                    x1s[i][:], psD[i][:], 0.5, xrt[tt][:, hq * 512:(hq + 1) * 512],
                                    OP.mult, OP.add))
                    block.vector(fve)
                    ev_done = sV.n

                    def fgp(gp, tg=tg, hh=hh, ev_done=ev_done):
                        sV.wait(gp, ev_done)
                        for tt in range(4):
                            for hq in range(2):
                                i = tt * 2 + hq
                                qG.inc(gp.dma_start(
                                    out=dstd[tg * 512 + tt * 128:tg * 512 + (tt + 1) * 128,
                                             hh * 1024 + hq * 512:hh * 1024 + (hq + 1) * 512],
                                    in_=x1s[i][:]), 16)
                    block.gpsimd(fgp)
            barrier()

        def qkv_phase(es):
            tokw = NT
            y1 = es.enter_context(nc.sbuf_tensor("y1", [128, NF * tokw], BF16))
            with ExitStack() as pes:
                prep_y(x1d, y1, 0, NTT, tokw, pes, "b")
                barrier()

            wq = [es.enter_context(nc.sbuf_tensor(f"wqb{i}", [128, 2048], BF16)) for i in range(2)]
            evq = [es.enter_context(nc.sbuf_tensor(f"evq{i}", [128, 4096], BF16)) for i in range(2)]
            vtmp = [es.enter_context(nc.sbuf_tensor(f"vtm{i}", [128, 4096], BF16)) for i in range(2)]
            psQ = [es.enter_context(nc.psum_tensor(f"psq{i}", [128, 512], F32)) for i in range(2)]
            psT2 = [es.enter_context(nc.psum_tensor(f"pst2{i}", [128, 512], BF16)) for i in range(2)]

            st_done = {}
            vt_done = {}
            ve_done = {}
            mm_done2 = {}
            ev_done2 = {}
            ld_done = [0] * 48

            for c in range(48):
                def fgl(gp, c=c):
                    if c >= 2:
                        sT.wait(gp, mm_done2[(c - 2, NG - 1)])
                    srcp = bass.AP(wqkv, c * 128, [[6144, 128], [6144 * 128, 16], [1, 128]])
                    dstp = bass.AP(wq[c % 2], 0, [[2048, 128], [128, 16], [1, 128]])
                    dB[c % 2].inc(gp.dma_start(out=dstp, in_=srcp), 16)
                    ld_done[c] = dB[c % 2].n
                block.gpsimd(fgl)

                for g in range(NG):
                    def ftm(t, c=c, g=g):
                        if g == 0:
                            dB[c % 2].wait(t, ld_done[c])
                        pg = g % 2
                        prev = (c, g - 2) if g >= 2 else ((c - 1, g + NG - 2) if c >= 1 else None)
                        if prev in ev_done2:
                            sS.wait(t, ev_done2[prev])
                        ins = None
                        for cf in range(NF):
                            ins = t.matmul(psQ[pg][:], wq[c % 2][:, cf * 128:(cf + 1) * 128],
                                           y1[:, cf * tokw + g * 512:cf * tokw + (g + 1) * 512],
                                           start=(cf == 0), stop=(cf == NF - 1))
                        sT.inc(ins)
                        mm_done2[(c, g)] = sT.n
                    block.tensor(ftm)

                    def fse(s, c=c, g=g):
                        sT.wait(s, mm_done2[(c, g)])
                        if g == 0 and c >= 2 and (c - 2) in st_done:
                            psem = dQ[(c - 2) % 2] if (c - 2) < 32 else dU[(c - 2) % 2]
                            psem.wait(s, st_done[c - 2])
                        sS.inc(s.copy(evq[c % 2][:, g * 512:(g + 1) * 512], psQ[g % 2][:]))
                        ev_done2[(c, g)] = sS.n
                    block.scalar(fse)

                    if c >= 32:
                        hV = c - 32

                        def ftv(t, c=c, g=g):
                            sS.wait(t, ev_done2[(c, g)])
                            prev = (c, g - 2) if g >= 2 else ((c - 1, g + NG - 2) if c > 32 else None)
                            if prev in ve_done:
                                sV.wait(t, ve_done[prev])
                            ins = None
                            for j in range(4):
                                ins = t.transpose(psT2[g % 2][:, j * 128:(j + 1) * 128],
                                                  evq[c % 2][:, g * 512 + j * 128:g * 512 + (j + 1) * 128], idt[:])
                            sT.inc(ins)
                            vt_done[(c, g)] = sT.n
                        block.tensor(ftv)

                        def fvv(v, c=c, g=g):
                            sT.wait(v, vt_done[(c, g)])
                            if g == 0 and c >= 34 and (c - 2) in st_done:
                                dU[c % 2].wait(v, st_done[c - 2])
                            sV.inc(v.tensor_copy(vtmp[c % 2][:, g * 512:(g + 1) * 512], psT2[g % 2][:]))
                            ve_done[(c, g)] = sV.n
                        block.vector(fvv)

                    if g == NG - 1:
                        if c < 32:
                            def fgs(gp, c=c):
                                dstt = qhd if c < 16 else khd
                                h2 = c % 16
                                sS.wait(gp, ev_done2[(c, NG - 1)])
                                dQ[c % 2].inc(gp.dma_start(out=dstt[h2][:, :], in_=evq[c % 2][:]), 16)
                                st_done[c] = dQ[c % 2].n
                            block.gpsimd(fgs)
                        else:
                            def fgs(gp, c=c, hV=hV):
                                sV.wait(gp, ve_done[(c, NG - 1)])
                                srcp = bass.AP(vtmp[c % 2], 0, [[4096, 128], [512, 8], [128, 4], [1, 128]])
                                dstp = bass.AP(vhd, hV * NT * 128,
                                               [[128, 128], [512 * 128, 8], [128 * 128, 4], [1, 128]])
                                dU[c % 2].inc(gp.dma_start(out=dstp, in_=srcp), 16)
                                st_done[c] = dU[c % 2].n
                            block.gpsimd(fgs)

            wg2 = es.enter_context(nc.sbuf_tensor("wg2", [128, NF * 16], BF16))
            sig = es.enter_context(nc.sbuf_tensor("sig", [16, 512], F32))
            psg = es.enter_context(nc.psum_tensor("psg", [16, 512], F32))
            psgT = es.enter_context(nc.psum_tensor("psgT", [128, 64], F32))

            def fsyg(gp):
                srcp = bass.AP(wgates, 0, [[16, 128], [16 * 128, 16], [1, 16]])
                dstp = bass.AP(wg2, 0, [[NF * 16, 128], [16, 16], [1, 16]])
                dC.inc(gp.dma_start(out=dstp, in_=srcp), 16)
            block.gpsimd(fsyg)
            gl_done = dC.n

            gv_done = 0
            for g in range(NG):
                def ftg(t, g=g, gv_done=gv_done):
                    dC.wait(t, gl_done)
                    if g >= 1:
                        sV.wait(t, gv_done)
                    ins = None
                    for cf in range(NF):
                        ins = t.matmul(psg[:], wg2[:, cf * 16:(cf + 1) * 16],
                                       y1[:, cf * tokw + g * 512:cf * tokw + (g + 1) * 512],
                                       start=(cf == 0), stop=(cf == NF - 1))
                    sT.inc(ins)
                block.tensor(ftg)
                mmg = sT.n

                def fsg(s, mmg=mmg):
                    sT.wait(s, mmg)
                    sS.inc(s.activation(sig[:], psg[:], AF.Sigmoid))
                block.scalar(fsg)
                sgd = sS.n

                def ftg2(t, sgd=sgd):
                    sS.wait(t, sgd)
                    ins = None
                    for j in range(4):
                        ins = t.transpose(psgT[:, j * 16:(j + 1) * 16],
                                          sig[:, j * 128:(j + 1) * 128], idtf[0:16, 0:16])
                    sT.inc(ins)
                block.tensor(ftg2)
                tgd = sT.n

                def fvg(v, g=g, tgd=tgd):
                    sT.wait(v, tgd)
                    dstp = bass.AP(gates_tm, (g * 4) * 16, [[NTT * H, 128], [16, 4], [1, 16]])
                    sV.inc(v.tensor_copy(dstp, psgT[:]))
                block.vector(fvg)
                gv_done = sV.n
            barrier()

        def attn_phase(es):
            wo = es.enter_context(nc.sbuf_tensor("wo", [128, NF * 2048], BF16))
            Kb = [es.enter_context(nc.sbuf_tensor(f"Kb{i}", [128, NT], BF16)) for i in range(2)]
            Vb = [es.enter_context(nc.sbuf_tensor(f"Vb{i}", [128, NT], BF16)) for i in range(2)]
            Qb = [es.enter_context(nc.sbuf_tensor(f"Qb{i}", [128, 512], BF16)) for i in range(2)]
            rowb = es.enter_context(nc.sbuf_tensor("rowb", [128, NT], F32))
            probr = es.enter_context(nc.sbuf_tensor("probr", [128, NT], BF16))
            probT = es.enter_context(nc.sbuf_tensor("probT", [128, 32 * 512], BF16))
            ofm = es.enter_context(nc.sbuf_tensor("ofm", [128, H * 512], BF16))
            x1t = [es.enter_context(nc.sbuf_tensor(f"ax{i}", [128, 2048], F32)) for i in range(2)]
            x2s = [es.enter_context(nc.sbuf_tensor(f"a2{i}", [128, 2048], F32)) for i in range(1)]
            lrow = es.enter_context(nc.sbuf_tensor("lrow", [128, 4], F32))
            srow = es.enter_context(nc.sbuf_tensor("srow", [128, 4], F32))
            psS = [es.enter_context(nc.psum_tensor(f"aS{i}", [128, 512], F32)) for i in range(2)]
            psPT = [es.enter_context(nc.psum_tensor(f"aT{i}", [128, 1024], BF16)) for i in range(2)]
            psSV = [es.enter_context(nc.psum_tensor(f"aV{i}", [128, 512], F32)) for i in range(2)]
            psO = [es.enter_context(nc.psum_tensor(f"aO{i}", [128, 512], F32)) for i in range(2)]

            def fwo(gp):
                for cc in range(NF):
                    dC.inc(gp.dma_start(out=wo[:, cc * 2048:(cc + 1) * 2048],
                                        in_=wout[cc * 128:(cc + 1) * 128, :]), 16)
            block.gpsimd(fwo)
            wo_done = dC.n

            for qb in range(NG):
                kmax = (qb + 1) * 512
                nj = kmax // 128
                load_done = [0] * H
                sv_done = [0] * H
                ofm_done = [0] * H

                def fvz(v, qb=qb):
                    sT.wait(v)
                    for qt in range(4):
                        for j in range(qb * 4 + qt + 1, qb * 4 + 4):
                            sV.inc(v.memset(probT[:, j * 512 + qt * 128:j * 512 + (qt + 1) * 128], 0))
                block.vector(fvz)

                for h in range(H):
                    p = h % 2

                    def fsy(sync, qb=qb, h=h, p=p, kmax=kmax):
                        if h >= 2:
                            sT.wait(sync, sv_done[h - 2])
                        dB[p].inc(sync.dma_start(out=Kb[p][:, 0:kmax], in_=khd[h][:, 0:kmax]), 16)
                        srcp = bass.AP(vhd, h * NT * 128, [[128, 128], [128 * 128, kmax // 128], [1, 128]])
                        dstp = bass.AP(Vb[p], 0, [[NT, 128], [128, kmax // 128], [1, 128]])
                        dB[p].inc(sync.dma_start(out=dstp, in_=srcp), 16)
                        dB[p].inc(sync.dma_start(out=Qb[p][:], in_=qhd[h][:, qb * 512:(qb + 1) * 512]), 16)
                        load_done[h] = dB[p].n
                    block.sync(fsy)

                    ex_done = [0] * 4
                    scl_done = [0] * 4
                    trv_done = [0] * 4
                    prev_tanh = []
                    for qt in range(4):
                        qi = qb * 4 + qt
                        kv = (qi + 1) * 128
                        nks = (kv + 511) // 512
                        tanh_done = [0] * nks
                        mmsc = [0] * nks

                        for ks in range(nks):
                            kw = min(512, kv - ks * 512)

                            def ftsc(t, h=h, p=p, qt=qt, ks=ks, kw=kw):
                                if qt == 0 and ks == 0:
                                    dB[h % 2].wait(t, load_done[h])
                                if ks >= 2:
                                    sS.wait(t, tanh_done[ks - 2])
                                elif qt >= 1 and ks < 2:
                                    sS.wait(t, prev_tanh[-2 + ks] if len(prev_tanh) >= 2 - ks else 0)
                                ins = t.matmul(psS[ks % 2][:, 0:kw], Qb[p][:, qt * 128:(qt + 1) * 128],
                                               Kb[p][:, ks * 512:ks * 512 + kw], start=True, stop=True)
                                sT.inc(ins)
                                mmsc[ks] = sT.n
                            block.tensor(ftsc)

                            def fssc(s, qt=qt, ks=ks, kw=kw):
                                sT.wait(s, mmsc[ks])
                                if qt >= 1 and ks == 0:
                                    sS.wait(s)
                                sS.inc(s.activation(rowb[:, ks * 512:ks * 512 + kw], psS[ks % 2][:, 0:kw],
                                                    AF.Tanh, scale=TANH_SC))
                                tanh_done[ks] = sS.n
                            block.scalar(fssc)
                        prev_tanh = tanh_done

                        def fvm(v, qt=qt, qi=qi):
                            sS.wait(v, tanh_done[nks - 1])
                            col = qi * 128
                            sV.inc(v.tensor_tensor(rowb[:, col:col + 128], rowb[:, col:col + 128],
                                                   msk[:], OP.add))
                        block.vector(fvm)
                        msk_done = sV.n

                        def fse(s, qt=qt, kv=kv, msk_done=msk_done):
                            sV.wait(s, msk_done)
                            if qt >= 1:
                                sV.wait(s, scl_done[qt - 1])
                            sS.inc(s.activation(probr[:, 0:kv], rowb[:, 0:kv], AF.Exp,
                                                bias=cb[:, 0:1], scale=50.0,
                                                accum_out=lrow[:, qt:qt + 1]))
                            ex_done[qt] = sS.n
                        block.scalar(fse)

                        def fvs(v, h=h, qt=qt, qi=qi, kv=kv):
                            sS.wait(v, ex_done[qt])
                            if qt >= 1:
                                sV.wait(v, trv_done[qt - 1])
                            v.reciprocal(srow[:, qt:qt + 1], lrow[:, qt:qt + 1])
                            sV.inc(v.tensor_tensor(srow[:, qt:qt + 1], srow[:, qt:qt + 1],
                                                   gates_tm[:, qi * 16 + h:qi * 16 + h + 1], OP.mult))
                            sV.wait(v)
                            sV.inc(v.tensor_scalar(probr[:, 0:kv], probr[:, 0:kv], srow[:, qt:qt + 1],
                                                   None, OP.mult))
                            scl_done[qt] = sV.n
                        block.vector(fvs)

                        nj2 = kv // 128
                        for jb in range(0, nj2, 8):
                            cnt = min(8, nj2 - jb)
                            pb = (jb // 8) % 2

                            def ftt(t, qt=qt, jb=jb, cnt=cnt, pb=pb):
                                if jb == 0:
                                    sV.wait(t, scl_done[qt])
                                sV.wait(t)
                                ins = None
                                for j2 in range(cnt):
                                    j = jb + j2
                                    ins = t.transpose(psPT[pb][:, j2 * 128:(j2 + 1) * 128],
                                                      probr[:, j * 128:(j + 1) * 128], idt[:])
                                sT.inc(ins)
                            block.tensor(ftt)
                            trb = sT.n

                            def fvt(v, qt=qt, jb=jb, cnt=cnt, pb=pb, trb=trb):
                                sT.wait(v, trb)
                                dstp = bass.AP(probT, jb * 512 + qt * 128,
                                               [[32 * 512, 128], [512, cnt], [1, 128]])
                                srcp = bass.AP(psPT[pb], 0, [[1024, 128], [128, cnt], [1, 128]])
                                sV.inc(v.tensor_copy(dstp, srcp))
                            block.vector(fvt)
                            trv_done[qt] = sV.n

                    def ftsv(t, qb=qb, h=h, p=p, nj=nj):
                        sV.wait(t)
                        if h >= 2:
                            sV.wait(t, ofm_done[h - 2])
                        psv = psSV[h % 2]
                        ins = None
                        for j in range(nj):
                            ins = t.matmul(psv[:], Vb[p][:, j * 128:(j + 1) * 128],
                                           probT[:, j * 512:(j + 1) * 512],
                                           start=(j == 0), stop=(j == nj - 1))
                        sT.inc(ins)
                        sv_done[h] = sT.n
                    block.tensor(ftsv)

                    def fvsv(v, h=h):
                        sT.wait(v, sv_done[h])
                        sV.inc(v.tensor_copy(ofm[:, h * 512:(h + 1) * 512], psSV[h % 2][:]))
                        ofm_done[h] = sV.n
                    block.vector(fvsv)

                for tt in range(4):
                    def fsyx(sync, qb=qb, tt=tt):
                        sV.wait(sync)
                        dA[tt % 2].inc(sync.dma_start(
                            out=x1t[tt % 2][:],
                            in_=x1d[qb * 512 + tt * 128:qb * 512 + (tt + 1) * 128, :]), 16)
                    block.sync(fsyx)
                    x1ld = dA[tt % 2].n

                    for ft4 in range(4):
                        idx = tt * 4 + ft4

                        def fto(t, qb=qb, ft4=ft4, tt=tt, idx=idx):
                            if idx == 0:
                                sV.wait(t)
                                dC.wait(t, wo_done)
                            if idx >= 2:
                                sV.wait(t)
                            po = idx % 2
                            ins = None
                            for h3 in range(H):
                                ins = t.matmul(psO[po][:],
                                               ofm[:, h3 * 512 + tt * 128:h3 * 512 + (tt + 1) * 128],
                                               wo[:, h3 * 2048 + ft4 * 512:h3 * 2048 + (ft4 + 1) * 512],
                                               start=(h3 == 0), stop=(h3 == H - 1))
                            sT.inc(ins)
                        block.tensor(fto)
                        opmm = sT.n

                        def fvx(v, idx=idx, tt=tt, ft4=ft4, opmm=opmm, x1ld=x1ld):
                            sT.wait(v, opmm)
                            if ft4 == 0:
                                dA[tt % 2].wait(v, x1ld)
                                qG.wait(v)
                            sV.inc(v.tensor_tensor(x2s[0][:, ft4 * 512:(ft4 + 1) * 512],
                                                   psO[idx % 2][:],
                                                   x1t[tt % 2][:, ft4 * 512:(ft4 + 1) * 512], OP.add))
                        block.vector(fvx)
                    x2ev = sV.n

                    def fgx(gp, qb=qb, tt=tt, x2ev=x2ev):
                        sV.wait(gp, x2ev)
                        qG.inc(gp.dma_start(
                            out=x2d[qb * 512 + tt * 128:qb * 512 + (tt + 1) * 128, :],
                            in_=x2s[0][:]), 16)
                    block.gpsimd(fgx)
            barrier()

        if phases >= 1:
            for hf in range(2):
                with ExitStack() as es1:
                    ffn_gate_up(hf, es1, layer=0)
        if phases >= 2:
            with ExitStack() as es2:
                ffn_down(x0, x1d, 0, es2)
        if phases >= 3:
            with ExitStack() as es3:
                qkv_phase(es3)
        if phases >= 4:
            with ExitStack() as es4:
                attn_phase(es4)
        if phases >= 5:
            for hf in range(2):
                with ExitStack() as es5:
                    ffn_gate_up(hf, es5, layer=1)
        if phases >= 6:
            with ExitStack() as es6:
                ffn_down(x2d, out, 1, es6)

        def ffin(gp):
            for sm in ALL:
                sm.wait(gp)
        block.gpsimd(ffin)

    return nc


_NC_CACHE = {}


def _get_nc(dbg=(), phases=6):
    key = (tuple(sorted(dbg)), phases)
    if key not in _NC_CACHE:
        _NC_CACHE[key] = build_nc(dbg, phases)
    return _NC_CACHE[key]


def _consts():
    ident16 = np.eye(128, dtype=ml_dtypes.bfloat16)
    identf = np.eye(128, dtype=np.float32)
    m = np.zeros((128, 128), dtype=np.float32)
    iu = np.triu_indices(128, 1)
    m[iu] = -1e5
    cbias = np.zeros((128, 8), dtype=np.float32)
    cbias[:, 0] = EXP_BIAS
    cbias[:, 1] = LN_EPS
    return ident16, identf, m, cbias


def kernel(L0_attn_x, L0_attn_w_qkv, L0_attn_w_gates, L0_attn_w_out,
           L0_ffn_gate_W_packed, L0_ffn_gate_scales, L0_ffn_up_W_packed,
           L0_ffn_up_scales, L0_ffn_down_W_packed, L0_ffn_down_scales,
           _dbg=(), _cores=None, _phases=6):
    x = np.asarray(L0_attn_x, dtype=np.float32)
    wqkv = np.ascontiguousarray(np.asarray(L0_attn_w_qkv, dtype=np.float32))
    wgates = np.ascontiguousarray(np.asarray(L0_attn_w_gates, dtype=np.float32))
    wout = np.ascontiguousarray(np.asarray(L0_attn_w_out, dtype=np.float32))
    gWa = np.ascontiguousarray(np.asarray(L0_ffn_gate_W_packed, dtype=np.int32))
    gSa = np.ascontiguousarray(np.asarray(L0_ffn_gate_scales, dtype=np.float32))
    uWa = np.ascontiguousarray(np.asarray(L0_ffn_up_W_packed, dtype=np.int32))
    uSa = np.ascontiguousarray(np.asarray(L0_ffn_up_scales, dtype=np.float32))
    dWa = np.ascontiguousarray(np.asarray(L0_ffn_down_W_packed, dtype=np.int32))
    dSa = np.ascontiguousarray(np.asarray(L0_ffn_down_scales, dtype=np.float32))
    ident16, identf, m, cbias = _consts()

    cores = list(range(NCORES)) if _cores is None else _cores
    nc = _get_nc(_dbg, _phases)
    in_maps = []
    for b in cores:
        in_maps.append({
            "x0": np.ascontiguousarray(x[b]),
            "wqkv": wqkv, "wgates": wgates, "wout": wout,
            "gW": gWa, "gS": gSa, "uW": uWa, "uS": uSa, "dW": dWa, "dS": dSa,
            "ident16": ident16, "identf": identf, "mask": m, "cbias": cbias,
        })
    res = run_bass_kernel_spmd(nc, in_maps, core_ids=list(range(len(cores))))
    if _dbg:
        return res
    return np.stack([r["out"] for r in res.results], axis=0)


# version 06 (final)
# speedup vs baseline: 1.2333x; 1.2333x over previous
from contextlib import ExitStack

import numpy as np
import ml_dtypes

import concourse.bass as bass
import concourse.mybir as mybir
from concourse.bass_utils import run_bass_kernel_spmd

F32 = mybir.dt.float32
BF16 = mybir.dt.bfloat16
I32 = mybir.dt.int32
OP = mybir.AluOpType
AF = mybir.ActivationFunctionType
AX = mybir.AxisListType

NT = 4096
D = 2048
NF = 16
MLP = 16384
NNC = 128
H = 16
DH = 128
NTT = 32
NG = 8
GELU_C1 = 0.7978845608028654
GELU_C2 = GELU_C1 * 0.044715
TANH_SC = (DH ** -0.5) / 50.0
EXP_BIAS = -30.0
LN_EPS = 1e-5
NCORES = 8
NK1 = 128
P1S = 5
P1EVAC = True
PREP_ONLY = False
QB_LIM = 8
H_LIM = 16


class S:
    def __init__(self, h):
        self.h = h
        self.n = 0

    def inc(self, ins, k=1):
        ins.then_inc(self.h, k)
        self.n += k
        return ins

    def wait(self, eng, tgt=None):
        t = self.n if tgt is None else tgt
        if t > 0:
            eng.wait_ge(self.h, t)


def interleave(*lists):
    n = max((len(l) for l in lists), default=0)
    for i in range(n):
        for l in lists:
            if i < len(l):
                l[i]()


def build_nc(dbg=(), phases=6):
    nc = bass.Bass("TRN2", target_bir_lowering=False, disable_frame_to_traceback=True)

    def inp(name, shape, dt):
        return nc.dram_tensor(name, shape, dt, kind="ExternalInput")

    def scratch(name, shape, dt):
        kind = "ExternalOutput" if name in dbg else "Internal"
        return nc.dram_tensor(name, shape, dt, kind=kind)

    x0 = inp("x0", [NT, D], F32)
    wqkv = inp("wqkv", [D, 3 * H * DH], F32)
    wgates = inp("wgates", [D, H], F32)
    wout = inp("wout", [D, D], F32)
    gW = inp("gW", [MLP, D // 2], I32)
    gS = inp("gS", [MLP, D // 32], F32)
    uW = inp("uW", [MLP, D // 2], I32)
    uS = inp("uS", [MLP, D // 32], F32)
    dW = inp("dW", [D, MLP // 2], I32)
    dS = inp("dS", [D, MLP // 32], F32)
    ident16 = inp("ident16", [128, 128], BF16)
    identf = inp("identf", [128, 128], F32)
    mask = inp("mask", [128, 128], F32)
    cbias = inp("cbias", [128, 8], F32)

    out = nc.dram_tensor("out", [NT, D], F32, kind="ExternalOutput")

    attn_only = (phases == 40)
    x1d = (inp if attn_only else scratch)("x1d", [NT, D], F32)
    x2d = scratch("x2d", [NT, D], F32)
    wgT = scratch("wgT", [NNC, 128, D], BF16)
    wuT = scratch("wuT", [NNC, 128, D], BF16)
    wdT = scratch("wdT", [NF, MLP, 128], BF16)
    interd = scratch("interd", [NG, 128, NNC * 512], BF16)
    qhd = (inp if attn_only else scratch)("qhd", [H, 128, NT], BF16)
    khd = (inp if attn_only else scratch)("khd", [H, 128, NT], BF16)
    vhd = (inp if attn_only else scratch)("vhd", [H, NT, 128], BF16)
    gatesd = inp("gatesd", [128, NTT * H], F32) if attn_only else None

    top = ExitStack()
    with top:
        idt = top.enter_context(nc.sbuf_tensor("idt", [128, 128], BF16))
        idtf = top.enter_context(nc.sbuf_tensor("idtf", [128, 128], F32))
        msk = top.enter_context(nc.sbuf_tensor("msk", [128, 128], F32))
        cb = top.enter_context(nc.sbuf_tensor("cb", [128, 8], F32))
        gates_tm = top.enter_context(nc.sbuf_tensor("gates_tm", [128, NTT * H], F32))

        qL = S(top.enter_context(nc.semaphore("qL")))
        qG = S(top.enter_context(nc.semaphore("qG")))
        sV = S(top.enter_context(nc.semaphore("sV")))
        sS = S(top.enter_context(nc.semaphore("sS")))
        sT = S(top.enter_context(nc.semaphore("sT")))
        qI = S(top.enter_context(nc.semaphore("qI")))
        dA = [S(top.enter_context(nc.semaphore(f"dA{i}"))) for i in range(2)]
        dB = [S(top.enter_context(nc.semaphore(f"dB{i}"))) for i in range(2)]
        dC = S(top.enter_context(nc.semaphore("dC")))
        dQ = [S(top.enter_context(nc.semaphore(f"dQ{i}"))) for i in range(4)]
        dU = [S(top.enter_context(nc.semaphore(f"dU{i}"))) for i in range(2)]
        ALL = [qL, qG, sV, sS, sT, qI] + dA + dB + [dC] + dQ + dU

        block = top.enter_context(nc.Block())

        def barrier():
            tgts = [(s, s.n) for s in ALL]
            for emit in (block.sync, block.gpsimd, block.vector, block.scalar, block.tensor):
                def f(eng, tgts=tgts):
                    for s, t in tgts:
                        s.wait(eng, t)
                emit(f)

        def f(sync):
            qL.inc(sync.dma_start(out=idt[:], in_=ident16.ap()), 16)
            qL.inc(sync.dma_start(out=idtf[:], in_=identf.ap()), 16)
            qL.inc(sync.dma_start(out=msk[:], in_=mask.ap()), 16)
            qL.inc(sync.dma_start(out=cb[:], in_=cbias.ap()), 16)
        block.sync(f)
        barrier()

        def prep_y(src, ybuf, t0, ntiles, tokw, es, tag):
            xt = [es.enter_context(nc.sbuf_tensor(f"px{tag}{i}", [128, D], F32)) for i in range(2)]
            sq = [es.enter_context(nc.sbuf_tensor(f"pq{tag}{i}", [128, D], F32)) for i in range(2)]
            yt = [es.enter_context(nc.sbuf_tensor(f"py{tag}{i}", [128, D], BF16)) for i in range(2)]
            st = es.enter_context(nc.sbuf_tensor(f"ps{tag}", [128, 16], F32))
            psT = [es.enter_context(nc.psum_tensor(f"pp{tag}{i}", [128, 1024], BF16)) for i in range(2)]

            load_done = [0] * ntiles
            stats_done = [0] * ntiles
            rstd_done = [0] * ntiles
            norm_done = [0] * ntiles
            tr_done = [[0, 0] for _ in range(ntiles)]
            evac_done = [[0, 0] for _ in range(ntiles)]

            for i in range(ntiles):
                p = i % 2
                trow = (t0 + i) * 128

                def fsy(sync, i=i, p=p, trow=trow):
                    if i >= 2:
                        sV.wait(sync, norm_done[i - 2])
                    dA[p].inc(sync.dma_start(out=xt[p][:], in_=src[trow:trow + 128, :]), 16)
                    load_done[i] = dA[p].n
                block.sync(fsy)

                def fv1(v, i=i, p=p):
                    dA[p].wait(v, load_done[i])
                    c = p * 8
                    v.scalar_tensor_tensor(sq[p][:], xt[p][:], 1.0, xt[p][:], OP.bypass, OP.mult,
                                           accum_out=st[:, c + 0:c + 1])
                    sV.inc(v.reduce_sum(st[:, c + 1:c + 2], xt[p][:], axis=AX.X))
                    sV.wait(v)
                    v.tensor_scalar(st[:, c + 2:c + 3], st[:, c + 1:c + 2], 1.0 / D, None, OP.mult)
                    sV.inc(v.tensor_scalar(st[:, c + 4:c + 5], st[:, c + 0:c + 1], 1.0 / D, None, OP.mult))
                    sV.wait(v)
                    sV.inc(v.tensor_scalar(st[:, c + 5:c + 6], st[:, c + 2:c + 3], st[:, c + 2:c + 3], None, OP.mult))
                    sV.wait(v)
                    sV.inc(v.tensor_tensor(st[:, c + 6:c + 7], st[:, c + 4:c + 5], st[:, c + 5:c + 6], OP.subtract))
                    stats_done[i] = sV.n
                block.vector(fv1)

                def fsc(s, i=i, p=p):
                    sV.wait(s, stats_done[i])
                    c = p * 8
                    sS.inc(s.activation(st[:, c + 7:c + 8], st[:, c + 6:c + 7], AF.Ln, bias=cb[:, 1:2]))
                    sS.wait(s)
                    sS.inc(s.activation(st[:, c + 7:c + 8], st[:, c + 7:c + 8], AF.Exp, scale=-0.5))
                    rstd_done[i] = sS.n
                block.scalar(fsc)

                def fv2(v, i=i, p=p):
                    sS.wait(v, rstd_done[i])
                    if i >= 2:
                        sT.wait(v, tr_done[i - 2][1])
                    c = p * 8
                    sV.inc(v.tensor_scalar(yt[p][:], xt[p][:], st[:, c + 2:c + 3], st[:, c + 7:c + 8],
                                           OP.subtract, OP.mult))
                    norm_done[i] = sV.n
                block.vector(fv2)

                def ft(t, i=i, p=p):
                    sV.wait(t, norm_done[i])
                    for half in range(2):
                        if i >= 1:
                            sV.wait(t, evac_done[i - 1][half])
                        ins = None
                        for c8 in range(8):
                            c = half * 8 + c8
                            ins = t.transpose(psT[half][:, c8 * 128:(c8 + 1) * 128],
                                              yt[p][:, c * 128:(c + 1) * 128], idt[:])
                        sT.inc(ins)
                        tr_done[i][half] = sT.n
                block.tensor(ft)

                def fv3(v, i=i, p=p):
                    for half in range(2):
                        sT.wait(v, tr_done[i][half])
                        cbase = half * 8
                        dst = bass.AP(ybuf, cbase * tokw + i * 128,
                                      [[NF * tokw, 128], [tokw, 8], [1, 128]])
                        srcp = bass.AP(psT[half], 0, [[1024, 128], [128, 8], [1, 128]])
                        sV.inc(v.tensor_copy(dst, srcp))
                        evac_done[i][half] = sV.n
                block.vector(fv3)

        def decode_thunks(v, w_ap, sct_ap, A, B, T, sc4, nw, waits=()):
            nsc = nw // 16
            def t0():
                for fn in waits:
                    fn(v)
                v.tensor_scalar(sc4[:], sct_ap, 4.0, None, OP.mult)
            def t1():
                v.tensor_scalar(A[:], w_ap, 7, 6, OP.bitwise_and, OP.logical_shift_left)
            def t2():
                v.tensor_scalar(B[:], w_ap, 0x70, 18, OP.bitwise_and, OP.logical_shift_left)
            def t3():
                v.tensor_tensor(A[:], A[:], B[:], OP.bitwise_or)
            def t4():
                v.tensor_scalar(A[:], A[:], 0x3E003E00, None, OP.bitwise_or)
            def t5():
                v.tensor_scalar(T[:], A[:].bitcast(BF16), 2.0, -0.25, OP.mult, OP.add)
            def t6():
                v.tensor_tensor(T[:], T[:], A[:].bitcast(BF16), OP.min)
            def t7():
                v.tensor_scalar(A[:], w_ap, 8, 12, OP.bitwise_and, OP.logical_shift_left)
            def t8():
                v.tensor_scalar(B[:], w_ap, 0x80, 24, OP.bitwise_and, OP.logical_shift_left)
            def t9():
                v.tensor_tensor(B[:], B[:], A[:], OP.bitwise_or)
            def t10():
                v.tensor_tensor(B[:], T[:].bitcast(I32), B[:], OP.bitwise_or)
            def t11():
                scb = bass.AP(sc4, 0, [[nsc, 128], [1, nsc], [0, 32]])
                sV.inc(v.tensor_tensor(T[:], B[:].bitcast(BF16), scb, OP.mult))
            return [t0, t1, t2, t3, t4, t5, t6, t7, t8, t9, t10, t11]

        def ffn_gate_up(hf, es, layer):
            xsrc = x0 if layer == 0 else x2d
            tokw = 2048
            ybuf = es.enter_context(nc.sbuf_tensor(f"y{layer}_{hf}", [128, NF * tokw], BF16))
            with ExitStack() as pes:
                prep_y(xsrc, ybuf, hf * 16, 16, tokw, pes, f"a{layer}{hf}")
                barrier()

            if PREP_ONLY:
                return
            dec = layer == 0
            if dec:
                gwp = [es.enter_context(nc.sbuf_tensor(f"gwp{layer}{hf}{i}", [128, 1024], I32)) for i in range(2)]
                gsp = [es.enter_context(nc.sbuf_tensor(f"gsp{layer}{hf}{i}", [128, 64], F32)) for i in range(2)]
                uwp = [es.enter_context(nc.sbuf_tensor(f"uwp{layer}{hf}{i}", [128, 1024], I32)) for i in range(2)]
                usp = [es.enter_context(nc.sbuf_tensor(f"usp{layer}{hf}{i}", [128, 64], F32)) for i in range(2)]
                dwp = [es.enter_context(nc.sbuf_tensor(f"dwp{layer}{hf}{i}", [128, 1024], I32)) for i in range(2)]
                dsp = [es.enter_context(nc.sbuf_tensor(f"dsp{layer}{hf}{i}", [128, 64], F32)) for i in range(2)]
                Ag = es.enter_context(nc.sbuf_tensor(f"Ag{layer}{hf}", [128, 1024], I32))
                Bg = es.enter_context(nc.sbuf_tensor(f"Bg{layer}{hf}", [128, 1024], I32))
                Au = es.enter_context(nc.sbuf_tensor(f"Au{layer}{hf}", [128, 1024], I32))
                Bu = es.enter_context(nc.sbuf_tensor(f"Bu{layer}{hf}", [128, 1024], I32))
                Ad = es.enter_context(nc.sbuf_tensor(f"Ad{layer}{hf}", [128, 1024], I32))
                Bd = es.enter_context(nc.sbuf_tensor(f"Bd{layer}{hf}", [128, 1024], I32))
                Tg = [es.enter_context(nc.sbuf_tensor(f"Tg{layer}{hf}{i}", [128, 2048], BF16)) for i in range(2)]
                Tu = [es.enter_context(nc.sbuf_tensor(f"Tu{layer}{hf}{i}", [128, 2048], BF16)) for i in range(2)]
                Td = [es.enter_context(nc.sbuf_tensor(f"Td{layer}{hf}{i}", [128, 2048], BF16)) for i in range(2)]
                sc4g = es.enter_context(nc.sbuf_tensor(f"s4g{layer}{hf}", [128, 64], F32))
                sc4u = es.enter_context(nc.sbuf_tensor(f"s4u{layer}{hf}", [128, 64], F32))
                sc4d = es.enter_context(nc.sbuf_tensor(f"s4d{layer}{hf}", [128, 64], F32))
                wdu = [es.enter_context(nc.sbuf_tensor(f"wdu{layer}{hf}{i}", [128, 2048], BF16)) for i in range(2)]
            wgsb = [es.enter_context(nc.sbuf_tensor(f"wgs{layer}{hf}{i}", [128, 2048], BF16)) for i in range(2)]
            wusb = [es.enter_context(nc.sbuf_tensor(f"wus{layer}{hf}{i}", [128, 2048], BF16)) for i in range(2)]
            gsb = [es.enter_context(nc.sbuf_tensor(f"gsb{layer}{hf}{i}", [128, 512], F32)) for i in range(2)]
            sqsb = [es.enter_context(nc.sbuf_tensor(f"sqb{layer}{hf}{i}", [128, 512], F32)) for i in range(2)]
            hbuf = [es.enter_context(nc.sbuf_tensor(f"hb{layer}{hf}{i}", [128, 512], F32)) for i in range(2)]
            tnh = [es.enter_context(nc.sbuf_tensor(f"th{layer}{hf}{i}", [128, 512], F32)) for i in range(2)]
            u1b = [es.enter_context(nc.sbuf_tensor(f"u1{layer}{hf}{i}", [128, 512], F32)) for i in range(2)]
            isb = [es.enter_context(nc.sbuf_tensor(f"ib{layer}{hf}{i}", [128, 2048], BF16)) for i in range(2)]
            psT = [es.enter_context(nc.psum_tensor(f"1t{layer}{hf}{i}", [128, 1024], BF16)) for i in range(2)]
            psG = [es.enter_context(nc.psum_tensor(f"1g{layer}{hf}{i}", [128, 512], F32)) for i in range(2)]
            psU = [es.enter_context(nc.psum_tensor(f"1u{layer}{hf}{i}", [128, 512], F32)) for i in range(2)]

            NK = NK1
            load_done = [0] * NK
            dec_done = [0] * NK
            trg_done = [dict() for _ in range(NK)]
            evac_done = [dict() for _ in range(NK)]
            wst_done = [0] * NK
            mm_done = [[0] * 4 for _ in range(NK)]
            b_done = [[0] * 4 for _ in range(NK)]
            th_done = [[0] * 4 for _ in range(NK)]
            gelu_done = [[0] * 4 for _ in range(NK)]
            ist_done = [[0] * 4 for _ in range(NK)]

            def kparams(k):
                return (k % 2, dec and (k % 2 == 0), (hf * 64 + k // 2) % NF,
                        (hf * 64 + k // 2) // NF, (k // 2) % 2)

            def emit_load(k):
                if k >= NK:
                    return
                p, do_d, hc, ws, dp = kparams(k)

                def fsy(sync, k=k, p=p, do_d=do_d, hc=hc, ws=ws, dp=dp):
                    if k >= 2:
                        if dec:
                            sV.wait(sync, dec_done[k - 2])
                        else:
                            sT.wait(sync, mm_done[k - 2][3])
                    if dec:
                        dB[p].inc(sync.dma_start(out=gwp[p][:], in_=gW[k * 128:(k + 1) * 128, :]), 16)
                        dB[p].inc(sync.dma_start(out=gsp[p][:], in_=gS[k * 128:(k + 1) * 128, :]), 16)
                        dB[p].inc(sync.dma_start(out=uwp[p][:], in_=uW[k * 128:(k + 1) * 128, :]), 16)
                        dB[p].inc(sync.dma_start(out=usp[p][:], in_=uS[k * 128:(k + 1) * 128, :]), 16)
                        if do_d:
                            dB[p].inc(sync.dma_start(out=dwp[dp][:],
                                                  in_=dW[hc * 128:(hc + 1) * 128, ws * 1024:(ws + 1) * 1024]), 16)
                            dB[p].inc(sync.dma_start(out=dsp[dp][:],
                                                  in_=dS[hc * 128:(hc + 1) * 128, ws * 64:(ws + 1) * 64]), 16)
                    else:
                        dB[p].inc(sync.dma_start(out=wgsb[p][:], in_=wgT[k]), 16)
                        dB[p].inc(sync.dma_start(out=wusb[p][:], in_=wuT[k]), 16)
                    load_done[k] = dB[p].n
                block.sync(fsy)

            def emit_decode(k):
                if not dec or k >= NK:
                    return
                p, do_d, hc, ws, dp = kparams(k)

                def fvd(v, k=k, p=p, do_d=do_d, dp=dp):
                    def w0(vv, k=k, p=p):
                        dB[p].wait(vv, load_done[k])
                        if k >= 1:
                            sT.wait(vv, trg_done[k - 1].get("last", 0))
                    chains = [
                        decode_thunks(v, gwp[p][:], gsp[p][:], Ag, Bg, Tg[p], sc4g, 1024, waits=(w0,)),
                        decode_thunks(v, uwp[p][:], usp[p][:], Au, Bu, Tu[p], sc4u, 1024),
                    ]
                    if do_d:
                        chains.append(decode_thunks(v, dwp[dp][:], dsp[dp][:], Ad, Bd, Td[dp], sc4d, 1024))
                    interleave(*chains)
                    dec_done[k] = sV.n
                block.vector(fvd)

            emit_load(0)
            emit_decode(0)
            for k in range(NK):
                p, do_d, hc, ws, dp = kparams(k)
                emit_load(k + 1)
                emit_decode(k + 1)

                v_base = sV.n
                b_done[k] = [v_base + 1, v_base + 2, v_base + 5, v_base + 6]
                gelu_done[k] = [v_base + 3, v_base + 4, v_base + 7, v_base + 8]
                s_base = sS.n
                if dec:
                    seq = [("g", 0), ("g", 1), ("u", 0), ("u", 1)] + \
                          ([("d", 0), ("d", 1)] if do_d else [])
                    for j, key in enumerate(seq):
                        evac_done[k][key] = s_base + j + 1
                    e0 = s_base + len(seq)
                    evac_done[k]["last2"] = evac_done[k].get(seq[-2], 0)
                    evac_done[k]["last1"] = evac_done[k].get(seq[-1], 0)
                else:
                    seq = []
                    e0 = s_base
                th_done[k] = [e0 + 5, e0 + 6, e0 + 11, e0 + 12]
                sq_pair = [e0 + 4, e0 + 10]
                s_end = e0 + 12

                if dec and P1S >= 2:
                    Tsrc = {"g": Tg[p], "u": Tu[p], "d": Td[dp] if do_d else None}

                    def ftt(t, k=k, seq=seq, Tsrc=Tsrc):
                        for j, (mname, half) in enumerate(seq):
                            Tm = Tsrc[mname]
                            if j == 0:
                                sV.wait(t, dec_done[k])
                            if j >= 2:
                                sS.wait(t, evac_done[k][seq[j - 2]])
                            elif k >= 1:
                                sS.wait(t, evac_done[k - 1].get("last2" if j == 0 else "last1", 0))
                            ins = None
                            for c8 in range(8):
                                c = half * 8 + c8
                                ins = t.transpose(psT[half][:, c8 * 128:(c8 + 1) * 128],
                                                  Tm[:, c * 128:(c + 1) * 128], idt[:])
                            sT.inc(ins)
                            trg_done[k][(mname, half)] = sT.n
                            trg_done[k]["last"] = sT.n
                    block.tensor(ftt)
                if P1S < 3:
                    continue

                def ftm(t, k=k, p=p):
                    if dec:
                        sS.wait(t, evac_done[k][("u", 1)])
                    else:
                        dB[p].wait(t, load_done[k])
                    for pr in range(2):
                        g4a, g4b = pr * 2, pr * 2 + 1
                        for g4 in (g4a, g4b):
                            if g4 >= 2:
                                sV.wait(t, gelu_done[k][g4 - 2])
                            elif k >= 1:
                                sV.wait(t, gelu_done[k - 1][g4 + 2])
                        ins = None
                        for c in range(NF):
                            for g4 in (g4a, g4b):
                                t.matmul(psG[g4 % 2][:], wgsb[p][:, c * 128:(c + 1) * 128],
                                         ybuf[:, c * tokw + g4 * 512:c * tokw + (g4 + 1) * 512],
                                         start=(c == 0), stop=(c == NF - 1))
                        for c in range(NF):
                            for g4 in (g4a, g4b):
                                ins = t.matmul(psU[g4 % 2][:], wusb[p][:, c * 128:(c + 1) * 128],
                                               ybuf[:, c * tokw + g4 * 512:c * tokw + (g4 + 1) * 512],
                                               start=(c == 0), stop=(c == NF - 1))
                        sT.inc(ins)
                        mm_done[k][g4a] = sT.n
                        mm_done[k][g4b] = sT.n
                block.tensor(ftm)
                if P1S < 4:
                    continue

                def fvg(v, k=k):
                    for pr in range(2):
                        g4a, g4b = pr * 2, pr * 2 + 1
                        pa, pb = g4a % 2, g4b % 2
                        sS.wait(v, sq_pair[pr])
                        v.tensor_scalar(hbuf[pa][:], sqsb[pa][:], GELU_C2, GELU_C1, OP.mult, OP.add)
                        v.tensor_scalar(hbuf[pb][:], sqsb[pb][:], GELU_C2, GELU_C1, OP.mult, OP.add)
                        sV.inc(v.tensor_tensor(hbuf[pa][:], hbuf[pa][:], gsb[pa][:], OP.mult))
                        sV.inc(v.tensor_tensor(hbuf[pb][:], hbuf[pb][:], gsb[pb][:], OP.mult))
                        sS.wait(v, th_done[k][g4a])
                        if k >= 2 and pr == 0:
                            dQ[k % 2].wait(v, ist_done[k - 2])
                        v.scalar_tensor_tensor(u1b[pa][:], tnh[pa][:], 1.0, gsb[pa][:], OP.add, OP.mult)
                        sS.wait(v, th_done[k][g4b])
                        v.scalar_tensor_tensor(u1b[pb][:], tnh[pb][:], 1.0, gsb[pb][:], OP.add, OP.mult)
                        sV.inc(v.scalar_tensor_tensor(isb[k % 2][:, g4a * 512:(g4a + 1) * 512],
                                                      u1b[pa][:], 0.5, psU[pa][:], OP.mult, OP.mult))
                        sV.inc(v.scalar_tensor_tensor(isb[k % 2][:, g4b * 512:(g4b + 1) * 512],
                                                      u1b[pb][:], 0.5, psU[pb][:], OP.mult, OP.mult))
                block.vector(fvg)
                assert sV.n == v_base + 8, (sV.n, v_base)

                if dec:
                    Edst = {"g": wgsb[p], "u": wusb[p], "d": wdu[dp] if do_d else None}

                def fsc(s2, k=k, p=p, seq=seq):
                    for j, (mname, half) in enumerate(seq):
                        dstbuf = Edst[mname]
                        if mname != "d" and k >= 2 and half == 0:
                            dU[k % 2].wait(s2, wst_done[k - 2])
                            sT.wait(s2, mm_done[k - 2][3])
                        if mname == "d" and k >= 4 and half == 0:
                            dU[k % 2].wait(s2, wst_done[k - 2])
                        sT.wait(s2, trg_done[k][(mname, half)])
                        sS.inc(s2.copy(dstbuf[:, half * 1024:(half + 1) * 1024], psT[half][:]))
                    for pr in range(2):
                        g4a, g4b = pr * 2, pr * 2 + 1
                        for g4 in (g4a, g4b):
                            pg = g4 % 2
                            sT.wait(s2, mm_done[k][g4])
                            if g4 >= 2:
                                sV.wait(s2, gelu_done[k][g4 - 2])
                            elif k >= 1:
                                sV.wait(s2, gelu_done[k - 1][g4 + 2])
                            sS.inc(s2.activation(sqsb[pg][:], psG[pg][:], AF.Square))
                            sS.inc(s2.copy(gsb[pg][:], psG[pg][:]))
                        for g4 in (g4a, g4b):
                            pg = g4 % 2
                            sV.wait(s2, b_done[k][g4])
                            sS.inc(s2.activation(tnh[pg][:], hbuf[pg][:], AF.Tanh))
                block.scalar(fsc)
                assert sS.n == s_end, (sS.n, s_end)

                def fgp(gp, k=k, p=p, do_d=do_d, hc=hc, ws=ws, dp=dp):
                    if P1S >= 5:
                        sV.wait(gp, gelu_done[k][3])
                        ip = k % 2
                        dstp = bass.AP(interd, (hf * 4) * NNC * 128 * 512 + k * 512,
                                       [[NNC * 512, 128], [NNC * 512 * 128, 4], [1, 512]])
                        srcp = bass.AP(isb[ip], 0, [[2048, 128], [512, 4], [1, 512]])
                        dQ[ip].inc(gp.dma_start(out=dstp, in_=srcp), 16)
                        ist_done[k] = dQ[ip].n
                    if dec:
                        sS.wait(gp, evac_done[k][("u", 1)])
                        dU[p].inc(gp.dma_start(out=wgT[k], in_=wgsb[p][:]), 16)
                        dU[p].inc(gp.dma_start(out=wuT[k], in_=wusb[p][:]), 16)
                        if do_d:
                            sS.wait(gp, evac_done[k][("d", 1)])
                            srcp = bass.AP(wdu[dp], 0, [[2048, 128], [128, 16], [1, 128]])
                            dstp = bass.AP(wdT, hc * MLP * 128 + (ws * 2048) * 128,
                                           [[128, 128], [128 * 128, 16], [1, 128]])
                            dU[p].inc(gp.dma_start(out=dstp, in_=srcp), 16)
                        wst_done[k] = dU[p].n
                block.gpsimd(fgp)

            barrier()

        def ffn_down(xres, dstd, layer, es):
            isb2 = es.enter_context(nc.sbuf_tensor(f"di{layer}", [128, NNC * 512], BF16))
            wdr = [es.enter_context(nc.sbuf_tensor(f"dw{layer}{i}", [128, 2048], BF16)) for i in range(2)]
            xrt = [es.enter_context(nc.sbuf_tensor(f"dx{layer}{i}", [128, 1024], F32)) for i in range(4)]
            x1s = [es.enter_context(nc.sbuf_tensor(f"dr{layer}{i}", [128, 512], F32)) for i in range(8)]
            psD = [es.enter_context(nc.psum_tensor(f"dp{layer}{i}", [128, 512], F32)) for i in range(8)]

            NK2 = NNC // 2
            prev_wdmm = None
            for tg in range(NG):
                def fsy0(sync, tg=tg):
                    if tg >= 1:
                        sT.wait(sync)
                    srcp = bass.AP(interd, tg * 128 * NNC * 512, [[NNC * 512, 128], [1, NNC * 512]])
                    dstp = bass.AP(isb2, 0, [[NNC * 512, 128], [1, NNC * 512]])
                    qI.inc(sync.dma_start(out=dstp, in_=srcp), 16)
                block.sync(fsy0)
                inter_done = qI.n

                for hh in range(2):
                    wdld = [0] * NK2
                    wdmm = [0] * NK2
                    pw = prev_wdmm
                    for k2 in range(NK2):
                        def fsy(sync, tg=tg, hh=hh, k2=k2, pw=pw):
                            if k2 >= 2:
                                sT.wait(sync, wdmm[k2 - 2])
                            elif pw is not None:
                                sT.wait(sync, pw[NK2 - 2 + k2])
                            for ksub in range(2):
                                srcp = bass.AP(wdT, (hh * 8) * MLP * 128 + (k2 * 2 + ksub) * 128 * 128,
                                               [[128, 128], [MLP * 128, 8], [1, 128]])
                                dstp = bass.AP(wdr[k2 % 2], ksub * 1024, [[2048, 128], [128, 8], [1, 128]])
                                dB[k2 % 2].inc(sync.dma_start(out=dstp, in_=srcp), 16)
                            wdld[k2] = dB[k2 % 2].n
                        block.sync(fsy)

                        def ftm(t, tg=tg, hh=hh, k2=k2, inter_done=inter_done):
                            if k2 == 0:
                                qI.wait(t, inter_done)
                                sV.wait(t)
                            dB[k2 % 2].wait(t, wdld[k2])
                            ins = None
                            for ksub in range(2):
                                k = k2 * 2 + ksub
                                for tt in range(4):
                                    for hq in range(2):
                                        ins = t.matmul(psD[tt * 2 + hq][:],
                                                       isb2[:, k * 512 + tt * 128:k * 512 + (tt + 1) * 128],
                                                       wdr[k2 % 2][:, ksub * 1024 + hq * 512:ksub * 1024 + (hq + 1) * 512],
                                                       start=(k == 0), stop=(k == NNC - 1))
                            sT.inc(ins)
                            wdmm[k2] = sT.n
                        block.tensor(ftm)
                    prev_wdmm = wdmm

                    def fsyx(sync, tg=tg, hh=hh):
                        if tg >= 1 or hh >= 1:
                            sV.wait(sync)
                        for tt in range(4):
                            dC.inc(sync.dma_start(
                                out=xrt[tt][:],
                                in_=xres[tg * 512 + tt * 128:tg * 512 + (tt + 1) * 128,
                                         hh * 1024:(hh + 1) * 1024]), 16)
                    block.sync(fsyx)
                    xr_done = dC.n

                    def fve(v, tg=tg, hh=hh, xr_done=xr_done):
                        sT.wait(v, wdmm[NK2 - 1])
                        dC.wait(v, xr_done)
                        qG.wait(v)
                        for tt in range(4):
                            for hq in range(2):
                                i = tt * 2 + hq
                                sV.inc(v.scalar_tensor_tensor(
                                    x1s[i][:], psD[i][:], 0.5, xrt[tt][:, hq * 512:(hq + 1) * 512],
                                    OP.mult, OP.add))
                    block.vector(fve)
                    ev_done = sV.n

                    def fgp(gp, tg=tg, hh=hh, ev_done=ev_done):
                        sV.wait(gp, ev_done)
                        for tt in range(4):
                            for hq in range(2):
                                i = tt * 2 + hq
                                qG.inc(gp.dma_start(
                                    out=dstd[tg * 512 + tt * 128:tg * 512 + (tt + 1) * 128,
                                             hh * 1024 + hq * 512:hh * 1024 + (hq + 1) * 512],
                                    in_=x1s[i][:]), 16)
                    block.gpsimd(fgp)
            barrier()

        def qkv_phase(es):
            tokw = NT
            y1 = es.enter_context(nc.sbuf_tensor("y1", [128, NF * tokw], BF16))
            with ExitStack() as pes:
                prep_y(x1d, y1, 0, NTT, tokw, pes, "b")
                barrier()

            wq = [es.enter_context(nc.sbuf_tensor(f"wqb{i}", [128, 2048], BF16)) for i in range(2)]
            evq = [es.enter_context(nc.sbuf_tensor(f"evq{i}", [128, 4096], BF16)) for i in range(2)]
            vtmp = [es.enter_context(nc.sbuf_tensor(f"vtm{i}", [128, 4096], BF16)) for i in range(2)]
            psQ = [es.enter_context(nc.psum_tensor(f"psq{i}", [128, 512], F32)) for i in range(2)]
            psT2 = [es.enter_context(nc.psum_tensor(f"pst2{i}", [128, 512], BF16)) for i in range(2)]

            st_done = {}
            vt_done = {}
            ve_done = {}
            mm_done2 = {}
            ev_done2 = {}
            ld_done = [0] * 48

            for c in range(48):
                def fgl(gp, c=c):
                    if c >= 2:
                        sT.wait(gp, mm_done2[(c - 2, NG - 1)])
                    srcp = bass.AP(wqkv, c * 128, [[6144, 128], [6144 * 128, 16], [1, 128]])
                    dstp = bass.AP(wq[c % 2], 0, [[2048, 128], [128, 16], [1, 128]])
                    dB[c % 2].inc(gp.dma_start(out=dstp, in_=srcp), 16)
                    ld_done[c] = dB[c % 2].n
                block.gpsimd(fgl)

                s_base = sS.n
                for g in range(NG):
                    ev_done2[(c, g)] = s_base + g + 1

                def ftm(t, c=c):
                    dB[c % 2].wait(t, ld_done[c])
                    for g2 in range(0, NG, 2):
                        for g in (g2, g2 + 1):
                            prev = (c, g - 2) if g >= 2 else ((c - 1, g + NG - 2) if c >= 1 else None)
                            if prev in ev_done2:
                                sS.wait(t, ev_done2[prev])
                        ins = None
                        for cf in range(NF):
                            for g in (g2, g2 + 1):
                                ins = t.matmul(psQ[g % 2][:], wq[c % 2][:, cf * 128:(cf + 1) * 128],
                                               y1[:, cf * tokw + g * 512:cf * tokw + (g + 1) * 512],
                                               start=(cf == 0), stop=(cf == NF - 1))
                        sT.inc(ins)
                        mm_done2[(c, g2)] = sT.n
                        mm_done2[(c, g2 + 1)] = sT.n
                block.tensor(ftm)

                def fsev2(s2, c=c):
                    for g in range(NG):
                        sT.wait(s2, mm_done2[(c, g)])
                        if g == 0 and c >= 2 and (c - 2) in st_done:
                            psem = dQ[(c - 2) % 2] if (c - 2) < 32 else dU[(c - 2) % 2]
                            psem.wait(s2, st_done[c - 2])
                        sS.inc(s2.copy(evq[c % 2][:, g * 512:(g + 1) * 512], psQ[g % 2][:]))
                block.scalar(fsev2)
                assert sS.n == s_base + NG

                if c >= 32:
                    hV = c - 32
                    for g in range(NG):
                        def ftv(t, c=c, g=g):
                            sS.wait(t, ev_done2[(c, g)])
                            prev = (c, g - 2) if g >= 2 else ((c - 1, g + NG - 2) if c > 32 else None)
                            if prev in ve_done:
                                sV.wait(t, ve_done[prev])
                            ins = None
                            for j in range(4):
                                ins = t.transpose(psT2[g % 2][:, j * 128:(j + 1) * 128],
                                                  evq[c % 2][:, g * 512 + j * 128:g * 512 + (j + 1) * 128], idt[:])
                            sT.inc(ins)
                            vt_done[(c, g)] = sT.n
                        block.tensor(ftv)

                        def fvv(v, c=c, g=g):
                            sT.wait(v, vt_done[(c, g)])
                            if g == 0 and c >= 34 and (c - 2) in st_done:
                                dU[c % 2].wait(v, st_done[c - 2])
                            sV.inc(v.tensor_copy(vtmp[c % 2][:, g * 512:(g + 1) * 512], psT2[g % 2][:]))
                            ve_done[(c, g)] = sV.n
                        block.vector(fvv)

                    def fgs(gp, c=c, hV=hV):
                        sV.wait(gp, ve_done[(c, NG - 1)])
                        srcp = bass.AP(vtmp[c % 2], 0, [[4096, 128], [512, 8], [128, 4], [1, 128]])
                        dstp = bass.AP(vhd, hV * NT * 128,
                                       [[128, 128], [512 * 128, 8], [128 * 128, 4], [1, 128]])
                        dU[c % 2].inc(gp.dma_start(out=dstp, in_=srcp), 16)
                        st_done[c] = dU[c % 2].n
                    block.gpsimd(fgs)
                else:
                    def fgs(gp, c=c):
                        dstt = qhd if c < 16 else khd
                        h2 = c % 16
                        sS.wait(gp, ev_done2[(c, NG - 1)])
                        dQ[c % 2].inc(gp.dma_start(out=dstt[h2][:, :], in_=evq[c % 2][:]), 16)
                        st_done[c] = dQ[c % 2].n
                    block.gpsimd(fgs)

            wg2 = es.enter_context(nc.sbuf_tensor("wg2", [128, NF * 16], BF16))
            sig = es.enter_context(nc.sbuf_tensor("sig", [16, 512], F32))
            psg = es.enter_context(nc.psum_tensor("psg", [16, 512], F32))
            psgT = es.enter_context(nc.psum_tensor("psgT", [128, 64], F32))

            def fsyg(gp):
                srcp = bass.AP(wgates, 0, [[16, 128], [16 * 128, 16], [1, 16]])
                dstp = bass.AP(wg2, 0, [[NF * 16, 128], [16, 16], [1, 16]])
                dC.inc(gp.dma_start(out=dstp, in_=srcp), 16)
            block.gpsimd(fsyg)
            gl_done = dC.n

            gv_done = 0
            for g in range(NG):
                def ftg(t, g=g, gv_done=gv_done):
                    dC.wait(t, gl_done)
                    if g >= 1:
                        sV.wait(t, gv_done)
                    ins = None
                    for cf in range(NF):
                        ins = t.matmul(psg[:], wg2[:, cf * 16:(cf + 1) * 16],
                                       y1[:, cf * tokw + g * 512:cf * tokw + (g + 1) * 512],
                                       start=(cf == 0), stop=(cf == NF - 1))
                    sT.inc(ins)
                block.tensor(ftg)
                mmg = sT.n

                def fsg(s, mmg=mmg):
                    sT.wait(s, mmg)
                    sS.inc(s.activation(sig[:], psg[:], AF.Sigmoid))
                block.scalar(fsg)
                sgd = sS.n

                def ftg2(t, sgd=sgd):
                    sS.wait(t, sgd)
                    ins = None
                    for j in range(4):
                        ins = t.transpose(psgT[:, j * 16:(j + 1) * 16],
                                          sig[:, j * 128:(j + 1) * 128], idtf[0:16, 0:16])
                    sT.inc(ins)
                block.tensor(ftg2)
                tgd = sT.n

                def fvg(v, g=g, tgd=tgd):
                    sT.wait(v, tgd)
                    dstp = bass.AP(gates_tm, (g * 4) * 16, [[NTT * H, 128], [16, 4], [1, 16]])
                    sV.inc(v.tensor_copy(dstp, psgT[:]))
                block.vector(fvg)
                gv_done = sV.n
            barrier()

        def attn_phase(es):
            wo = es.enter_context(nc.sbuf_tensor("wo", [128, NF * 2048], BF16))
            Kb = [es.enter_context(nc.sbuf_tensor(f"Kb{i}", [128, NT], BF16)) for i in range(2)]
            Vb = [es.enter_context(nc.sbuf_tensor(f"Vb{i}", [128, NT], BF16)) for i in range(2)]
            Qb = [es.enter_context(nc.sbuf_tensor(f"Qb{i}", [128, 512], BF16)) for i in range(2)]
            rowb = es.enter_context(nc.sbuf_tensor("rowb", [128, NT], F32))
            probr = es.enter_context(nc.sbuf_tensor("probr", [128, NT], BF16))
            probT = es.enter_context(nc.sbuf_tensor("probT", [128, 32 * 512], BF16))
            ofm = es.enter_context(nc.sbuf_tensor("ofm", [128, H * 512], BF16))
            x1t = [es.enter_context(nc.sbuf_tensor(f"ax{i}", [128, 2048], F32)) for i in range(2)]
            x2s = [es.enter_context(nc.sbuf_tensor(f"a2{i}", [128, 2048], F32)) for i in range(1)]
            lrow = es.enter_context(nc.sbuf_tensor("lrow", [128, 4], F32))
            srow = es.enter_context(nc.sbuf_tensor("srow", [128, 4], F32))
            psS = [es.enter_context(nc.psum_tensor(f"aS{i}", [128, 512], F32)) for i in range(2)]
            psPT = [es.enter_context(nc.psum_tensor(f"aT{i}", [128, 1024], BF16)) for i in range(2)]
            psSV = [es.enter_context(nc.psum_tensor(f"aV{i}", [128, 512], F32)) for i in range(2)]
            psO = [es.enter_context(nc.psum_tensor(f"aO{i}", [128, 512], F32)) for i in range(2)]

            def fwo(gp):
                for cc in range(NF):
                    dC.inc(gp.dma_start(out=wo[:, cc * 2048:(cc + 1) * 2048],
                                        in_=wout[cc * 128:(cc + 1) * 128, :]), 16)
            block.gpsimd(fwo)
            wo_done = dC.n

            for qb in range(min(NG, QB_LIM)):
                kmax = (qb + 1) * 512
                nj = kmax // 128
                load_done = [0] * H
                sv_done = [0] * H
                ofm_done = [0] * H
                prev_tail = [0, 0]

                def fvz(v, qb=qb):
                    sT.wait(v)
                    for qt in range(4):
                        for j in range(qb * 4 + qt + 1, qb * 4 + 4):
                            sV.inc(v.memset(probT[:, j * 512 + qt * 128:j * 512 + (qt + 1) * 128], 0))
                block.vector(fvz)

                for h in range(min(H, H_LIM)):
                    p = h % 2

                    def fsy(sync, qb=qb, h=h, p=p, kmax=kmax):
                        if h >= 2:
                            sT.wait(sync, sv_done[h - 2])
                        dB[p].inc(sync.dma_start(out=Kb[p][:, 0:kmax], in_=khd[h][:, 0:kmax]), 16)
                        srcp = bass.AP(vhd, h * NT * 128, [[128, 128], [128 * 128, kmax // 128], [1, 128]])
                        dstp = bass.AP(Vb[p], 0, [[NT, 128], [128, kmax // 128], [1, 128]])
                        dB[p].inc(sync.dma_start(out=dstp, in_=srcp), 16)
                        dB[p].inc(sync.dma_start(out=Qb[p][:], in_=qhd[h][:, qb * 512:(qb + 1) * 512]), 16)
                        load_done[h] = dB[p].n
                    block.sync(fsy)

                    s_base = sS.n
                    tanh_t = {}
                    ex_t = [0] * 4
                    sc_seq = []
                    ti = s_base
                    for qt in range(4):
                        kv = (qb * 4 + qt + 1) * 128
                        nks = (kv + 511) // 512
                        for ks in range(nks):
                            sc_seq.append((qt, ks))
                            ti += 1
                            tanh_t[(qt, ks)] = ti
                        ti += 1
                        ex_t[qt] = ti
                    s_end = ti

                    scl_done = [0] * 4
                    trv_done = [0] * 4
                    msk_done = [0] * 4
                    mmsc = {}

                    flat_idx = [0]

                    def emit_scores(qt):
                        kv = (qb * 4 + qt + 1) * 128
                        nks = (kv + 511) // 512

                        def ftsc(t, qt=qt, kv=kv, nks=nks, h=h, p=p):
                            for ks in range(nks):
                                idx = flat_idx[0] + ks
                                kw = min(512, kv - ks * 512)
                                if idx == 0:
                                    dB[p].wait(t, load_done[h])
                                if idx >= 2:
                                    sS.wait(t, tanh_t[sc_seq[idx - 2]])
                                else:
                                    sS.wait(t, max(prev_tail))
                                ins = t.matmul(psS[idx % 2][:, 0:kw], Qb[p][:, qt * 128:(qt + 1) * 128],
                                               Kb[p][:, ks * 512:ks * 512 + kw], start=True, stop=True)
                                sT.inc(ins)
                                mmsc[(qt, ks)] = (sT.n, idx % 2, kw)
                            flat_idx[0] += nks
                        block.tensor(ftsc)

                    def emit_trans(qt):
                        kv = (qb * 4 + qt + 1) * 128
                        nj2 = kv // 128
                        for jb in range(0, nj2, 8):
                            cnt = min(8, nj2 - jb)
                            pb = (jb // 8) % 2

                            def ftt(t, qt=qt, jb=jb, cnt=cnt, pb=pb):
                                if jb == 0:
                                    sV.wait(t, scl_done[qt])
                                sV.wait(t)
                                ins = None
                                for j2 in range(cnt):
                                    j = jb + j2
                                    ins = t.transpose(psPT[pb][:, j2 * 128:(j2 + 1) * 128],
                                                      probr[:, j * 128:(j + 1) * 128], idt[:])
                                sT.inc(ins)
                            block.tensor(ftt)
                            trb = sT.n

                            def fvt(v, qt=qt, jb=jb, cnt=cnt, pb=pb, trb=trb):
                                sT.wait(v, trb)
                                dstp = bass.AP(probT, jb * 512 + qt * 128,
                                               [[32 * 512, 128], [512, cnt], [1, 128]])
                                srcp = bass.AP(psPT[pb], 0, [[1024, 128], [128, cnt], [1, 128]])
                                sV.inc(v.tensor_copy(dstp, srcp))
                            block.vector(fvt)
                            trv_done[qt] = sV.n

                    for qt in range(4):
                        qi = qb * 4 + qt
                        kv = (qi + 1) * 128
                        emit_scores(qt)
                        if qt >= 1:
                            emit_trans(qt - 1)

                        def fvm(v, qt=qt, qi=qi):
                            sS.wait(v, tanh_t[max((k2 for k2 in tanh_t if k2[0] == qt))])
                            col = qi * 128
                            sV.inc(v.tensor_tensor(rowb[:, col:col + 128], rowb[:, col:col + 128],
                                                   msk[:], OP.add))
                        block.vector(fvm)
                        msk_done[qt] = sV.n

                        def fvs(v, h=h, qt=qt, qi=qi, kv=kv):
                            sS.wait(v, ex_t[qt])
                            if qt >= 1:
                                sV.wait(v, trv_done[qt - 1])
                            sV.inc(v.reciprocal(srow[:, qt:qt + 1], lrow[:, qt:qt + 1]))
                            sV.wait(v)
                            sV.inc(v.tensor_tensor(srow[:, qt:qt + 1], srow[:, qt:qt + 1],
                                                   gates_tm[:, qi * 16 + h:qi * 16 + h + 1], OP.mult))
                            sV.wait(v)
                            sV.inc(v.tensor_scalar(probr[:, 0:kv], probr[:, 0:kv], srow[:, qt:qt + 1],
                                                   None, OP.mult))
                        block.vector(fvs)
                        scl_done[qt] = sV.n
                    emit_trans(3)

                    def fsb(s2, qb=qb, sc_seq=sc_seq):
                        for qt in range(4):
                            qi = qb * 4 + qt
                            kv = (qi + 1) * 128
                            for ks in range((kv + 511) // 512):
                                tgt, bank, kw = mmsc[(qt, ks)]
                                sT.wait(s2, tgt)
                                if qt >= 1 and ks == 0:
                                    sS.wait(s2)
                                sS.inc(s2.activation(rowb[:, ks * 512:ks * 512 + kw],
                                                     psS[bank][:, 0:kw], AF.Tanh, scale=TANH_SC))
                            sV.wait(s2, msk_done[qt])
                            if qt >= 1:
                                sV.wait(s2, scl_done[qt - 1])
                            sS.inc(s2.activation(probr[:, 0:kv], rowb[:, 0:kv], AF.Exp,
                                                 bias=cb[:, 0:1], scale=50.0,
                                                 accum_out=lrow[:, qt:qt + 1]))
                    block.scalar(fsb)
                    assert sS.n == s_end, (sS.n, s_end)
                    prev_tail = [tanh_t[sc_seq[-2]], tanh_t[sc_seq[-1]]]

                    def ftsv(t, qb=qb, h=h, p=p, nj=nj):
                        sV.wait(t)
                        if h >= 2:
                            sV.wait(t, ofm_done[h - 2])
                        psv = psSV[h % 2]
                        ins = None
                        for j in range(nj):
                            ins = t.matmul(psv[:], Vb[p][:, j * 128:(j + 1) * 128],
                                           probT[:, j * 512:(j + 1) * 512],
                                           start=(j == 0), stop=(j == nj - 1))
                        sT.inc(ins)
                        sv_done[h] = sT.n
                    block.tensor(ftsv)

                    def fvsv(v, h=h):
                        sT.wait(v, sv_done[h])
                        sV.inc(v.tensor_copy(ofm[:, h * 512:(h + 1) * 512], psSV[h % 2][:]))
                        ofm_done[h] = sV.n
                    block.vector(fvsv)

                for tt in range(4):
                    def fsyx(sync, qb=qb, tt=tt):
                        sV.wait(sync)
                        dA[tt % 2].inc(sync.dma_start(
                            out=x1t[tt % 2][:],
                            in_=x1d[qb * 512 + tt * 128:qb * 512 + (tt + 1) * 128, :]), 16)
                    block.sync(fsyx)
                    x1ld = dA[tt % 2].n

                    for fp in (0, 2):
                        def fto(t, qb=qb, fp=fp, tt=tt):
                            if tt == 0 and fp == 0:
                                sV.wait(t)
                                dC.wait(t, wo_done)
                            else:
                                sV.wait(t)
                            ins = None
                            for h3 in range(H):
                                for fi in range(2):
                                    ins = t.matmul(psO[fi][:],
                                                   ofm[:, h3 * 512 + tt * 128:h3 * 512 + (tt + 1) * 128],
                                                   wo[:, h3 * 2048 + (fp + fi) * 512:h3 * 2048 + (fp + fi + 1) * 512],
                                                   start=(h3 == 0), stop=(h3 == H - 1))
                            sT.inc(ins)
                        block.tensor(fto)
                        opmm = sT.n

                        def fvx(v, tt=tt, fp=fp, opmm=opmm, x1ld=x1ld):
                            sT.wait(v, opmm)
                            if fp == 0:
                                dA[tt % 2].wait(v, x1ld)
                                qG.wait(v)
                            for fi in range(2):
                                sV.inc(v.tensor_tensor(x2s[0][:, (fp + fi) * 512:(fp + fi + 1) * 512],
                                                       psO[fi][:],
                                                       x1t[tt % 2][:, (fp + fi) * 512:(fp + fi + 1) * 512], OP.add))
                        block.vector(fvx)
                    x2ev = sV.n

                    def fgx(gp, qb=qb, tt=tt, x2ev=x2ev):
                        sV.wait(gp, x2ev)
                        qG.inc(gp.dma_start(
                            out=x2d[qb * 512 + tt * 128:qb * 512 + (tt + 1) * 128, :],
                            in_=x2s[0][:]), 16)
                    block.gpsimd(fgx)
            barrier()

        if attn_only:
            def fgl0(sync):
                qL.inc(sync.dma_start(out=gates_tm[:], in_=gatesd.ap()), 16)
            block.sync(fgl0)
            barrier()
            with ExitStack() as esA:
                attn_phase(esA)
            def ffinA(gp):
                for sm in ALL:
                    sm.wait(gp)
            block.gpsimd(ffinA)
            return nc
        if phases >= 1:
            for hf in range(2):
                with ExitStack() as es1:
                    ffn_gate_up(hf, es1, layer=0)
        if phases >= 2:
            with ExitStack() as es2:
                ffn_down(x0, x1d, 0, es2)
        if phases >= 3:
            with ExitStack() as es3:
                qkv_phase(es3)
        if phases >= 4:
            with ExitStack() as es4:
                attn_phase(es4)
        if phases >= 5:
            for hf in range(2):
                with ExitStack() as es5:
                    ffn_gate_up(hf, es5, layer=1)
        if phases >= 6:
            with ExitStack() as es6:
                ffn_down(x2d, out, 1, es6)

        def ffin(gp):
            for sm in ALL:
                sm.wait(gp)
        block.gpsimd(ffin)

    return nc


_NC_CACHE = {}


def _get_nc(dbg=(), phases=6):
    key = (tuple(sorted(dbg)), phases)
    if key not in _NC_CACHE:
        _NC_CACHE[key] = build_nc(dbg, phases)
    return _NC_CACHE[key]


def _consts():
    ident16 = np.eye(128, dtype=ml_dtypes.bfloat16)
    identf = np.eye(128, dtype=np.float32)
    m = np.zeros((128, 128), dtype=np.float32)
    iu = np.triu_indices(128, 1)
    m[iu] = -1e5
    cbias = np.zeros((128, 8), dtype=np.float32)
    cbias[:, 0] = EXP_BIAS
    cbias[:, 1] = LN_EPS
    return ident16, identf, m, cbias


def kernel(L0_attn_x, L0_attn_w_qkv, L0_attn_w_gates, L0_attn_w_out,
           L0_ffn_gate_W_packed, L0_ffn_gate_scales, L0_ffn_up_W_packed,
           L0_ffn_up_scales, L0_ffn_down_W_packed, L0_ffn_down_scales,
           _dbg=(), _cores=None, _phases=6):
    x = np.asarray(L0_attn_x, dtype=np.float32)
    wqkv = np.ascontiguousarray(np.asarray(L0_attn_w_qkv, dtype=np.float32))
    wgates = np.ascontiguousarray(np.asarray(L0_attn_w_gates, dtype=np.float32))
    wout = np.ascontiguousarray(np.asarray(L0_attn_w_out, dtype=np.float32))
    gWa = np.ascontiguousarray(np.asarray(L0_ffn_gate_W_packed, dtype=np.int32))
    gSa = np.ascontiguousarray(np.asarray(L0_ffn_gate_scales, dtype=np.float32))
    uWa = np.ascontiguousarray(np.asarray(L0_ffn_up_W_packed, dtype=np.int32))
    uSa = np.ascontiguousarray(np.asarray(L0_ffn_up_scales, dtype=np.float32))
    dWa = np.ascontiguousarray(np.asarray(L0_ffn_down_W_packed, dtype=np.int32))
    dSa = np.ascontiguousarray(np.asarray(L0_ffn_down_scales, dtype=np.float32))
    ident16, identf, m, cbias = _consts()

    cores = list(range(NCORES)) if _cores is None else _cores
    nc = _get_nc(_dbg, _phases)
    in_maps = []
    for b in cores:
        in_maps.append({
            "x0": np.ascontiguousarray(x[b]),
            "wqkv": wqkv, "wgates": wgates, "wout": wout,
            "gW": gWa, "gS": gSa, "uW": uWa, "uS": uSa, "dW": dWa, "dS": dSa,
            "ident16": ident16, "identf": identf, "mask": m, "cbias": cbias,
        })
    res = run_bass_kernel_spmd(nc, in_maps, core_ids=list(range(len(cores))))
    if _dbg:
        return res
    return np.stack([r["out"] for r in res.results], axis=0)
